# version 02 (final)
# baseline (speedup 1.0000x reference)
import numpy as np
import ml_dtypes
from contextlib import ExitStack

import concourse.bass as bass
import concourse.mybir as mybir
from concourse.bass_utils import run_bass_kernel_spmd

F32 = mybir.dt.float32
BF16 = mybir.dt.bfloat16
U8 = mybir.dt.uint8
AF = mybir.ActivationFunctionType
ALU = mybir.AluOpType
AX = mybir.AxisListType
BF = ml_dtypes.bfloat16

NCORES = 8
D = 1024
T = 2048
NCH = D // 128
EPS = 1e-5


DBG = False


def cdiv(a, b):
    return -(-a // b)


class V3:

    def __init__(self, t, d1, d2, pitch=None, off=0, base=0):
        self.t, self.d1, self.d2 = t, d1, d2
        self.pitch = d2 if pitch is None else pitch
        self.off = off + base

    def __getitem__(self, idx):
        p, c, s = idx
        if isinstance(s, int):
            s = slice(s, s + 1)
        if isinstance(c, int):
            s0 = 0 if s.start is None else s.start
            s1 = self.d2 if s.stop is None else s.stop
            off = c * self.pitch + self.off
            return self.t[p, off + s0: off + s1]
        c0 = 0 if c.start is None else c.start
        c1 = self.d1 if c.stop is None else c.stop
        s0 = 0 if s.start is None else s.start
        s1 = self.d2 if s.stop is None else s.stop
        p0 = 0 if p.start is None else p.start
        p1 = 128 if p.stop is None else p.stop
        shp = list(self.t.shape)
        ppitch = 1
        for d in shp[1:]:
            ppitch *= int(d)
        return bass.AP(self.t, p0 * ppitch + c0 * self.pitch + self.off + s0,
                       [[ppitch, p1 - p0], [self.pitch, c1 - c0], [1, s1 - s0]])


def bview(tile, nch, Tn, pitch=None):
    return bass.AP(tile, 0, [[pitch if pitch is not None else Tn, 128],
                             [0, nch], [1, Tn]])


class P:

    ENGINES = ("tensor", "vector", "scalar", "gpsimd", "sync")

    def __init__(self, name="prog"):
        self.nc = bass.Bass("TRN2", target_bir_lowering=False, debug=False)
        self.es = ExitStack()
        self.steps = []
        self.n = 0
        self.t = {}

    def hbm_in(self, name, shape, dtype=F32):
        ap = self.nc.dram_tensor(name, list(shape), dtype, kind="ExternalInput").ap()
        self.t[name] = ap
        return ap

    def hbm_out(self, name, shape, dtype=F32):
        ap = self.nc.dram_tensor(name, list(shape), dtype, kind="ExternalOutput").ap()
        self.t[name] = ap
        return ap

    def hbm_tmp(self, name, shape, dtype=F32):
        ap = self.nc.dram_tensor(name, list(shape), dtype).ap()
        self.t[name] = ap
        return ap

    def sbuf(self, name, shape, dtype):
        h = self.es.enter_context(self.nc.sbuf_tensor(name, list(shape), dtype))
        self.t[name] = h
        return h

    def psum(self, name, shape, dtype=F32):
        h = self.es.enter_context(self.nc.psum_tensor(name, list(shape), dtype))
        self.t[name] = h
        return h

    def op(self, eng, fn, inc=1):
        self.steps.append((eng, fn, self.n, inc))
        self.n += inc

    def dump(self, name, ap, shape, dtype=F32):
        h = self.hbm_out(name, shape, dtype)
        self.dma("sync", h, ap)

    def dma(self, eng, out, in_, **kw):
        self.op(eng, lambda e: e.dma_start(out=out, in_=in_, **kw), inc=16)

    def mm(self, out, lhsT, rhs, start=True, stop=True):
        self.op("tensor", lambda e: e.matmul(out, lhsT, rhs, start=start, stop=stop))

    def mm_chain(self, psl, pairs):
        nk = len(pairs)
        for i, (l, r) in enumerate(pairs):
            self.op("tensor", lambda e, l=l, r=r, i=i: e.matmul(
                psl, l, r, start=(i == 0), stop=(i == nk - 1)))

    def finish(self):
        total = self.n
        S = self.es.enter_context(self.nc.semaphore("S"))
        with self.nc.Block() as blk:
            def run_for(eng_name, eng):
                for engine, fn, wait, inc in self.steps:
                    if engine != eng_name:
                        continue
                    if wait > 0:
                        eng.wait_ge(S, wait)
                    instr = fn(eng)
                    instr.then_inc(S, inc)
                if eng_name == "sync":
                    eng.wait_ge(S, total)

            for name in self.ENGINES:
                getattr(blk, name)(lambda eng, _n=name: run_for(_n, eng))
        self.es.close()
        return self.nc


def load_w(p, name, hbm, K, M, m0=0, m1=None, sb=None, dtype=BF16):
    if m1 is None:
        m1 = M
    nkc = cdiv(K, 128)
    if sb is None:
        sb = p.sbuf(name, [128, nkc, m1 - m0], dtype)
    for kc in range(nkc):
        rows = min(128, K - kc * 128)
        src = bass.AP(hbm.tensor, hbm.offset + (kc * 128) * M + m0,
                      [[M, rows], [1, m1 - m0]])
        p.dma("gpsimd", sb[0:rows, kc, 0:m1 - m0], src)
    return sb


def load_col(p, name, hbm, M, dtype=F32):
    nmc = cdiv(M, 128)
    sb = p.sbuf(name, [128, nmc], dtype)
    for mc in range(nmc):
        rows = min(128, M - mc * 128)
        src = bass.AP(hbm.tensor, hbm.offset + mc * 128, [[1, rows], [1, 1]])
        p.dma("gpsimd" if dtype != F32 else "sync", sb[0:rows, mc:mc + 1], src)
    return sb


def linear(p, tag, w_hbm, K, M, x, dst, NT, wbuf, ps,
           bias=None, act=None, act_scale=1.0, x_c0=0, dst_c0=0, tok0=0,
           n_tile=512, mblk=512, dst_fn=None, extra_rhs=None, dst_tok0=None,
           accum=False, rhs_fn=None):
    if dst_tok0 is None:
        dst_tok0 = tok0
    """w_hbm: HBM [K, M] (lhsT layout, contract dim major).
    x: V3 fm view; dst: V3 fm view (dest rows dst_c0 + mc) or dst_fn(mc, mrows, t0, tw)->AP.
    wbuf: sbuf [128, cdiv(K,128), >=mblk]; ps: psum [128, >=n_tile].
    extra_rhs: optional (w2_hbm, K2, x2, x2_c0) accumulated into the same psum."""
    nkc = cdiv(K, 128)
    for mb0 in range(0, M, mblk):
        mb1 = min(M, mb0 + mblk)
        load_w(p, None, w_hbm, K, M, m0=mb0, m1=mb1, sb=wbuf)
        if extra_rhs is not None:
            w2_hbm, K2, x2, x2_c0, wbuf2 = extra_rhs
            load_w(p, None, w2_hbm, K2, M, m0=mb0, m1=mb1, sb=wbuf2)
        for mc0 in range(mb0, mb1, 128):
            mc = mc0 // 128
            mrows = min(128, M - mc0)
            for t0 in range(0, NT, n_tile):
                tw = min(n_tile, NT - t0)
                psl = ps[0:mrows, 0:tw]
                nk2 = nkc if extra_rhs is None else nkc + cdiv(extra_rhs[1], 128)
                ki = 0
                for kc in range(nkc):
                    krows = min(128, K - kc * 128)
                    if rhs_fn is not None:
                        rr = rhs_fn(kc, t0, tw)
                    else:
                        rr = x[0:krows, x_c0 + kc, tok0 + t0: tok0 + t0 + tw]
                    p.op("tensor", lambda e, kc=kc, krows=krows, mc0=mc0, mb0=mb0,
                         t0=t0, tw=tw, psl=psl, mrows=mrows, ki=ki, rr=rr: e.matmul(
                             psl,
                             wbuf[0:krows, kc, mc0 - mb0: mc0 - mb0 + mrows],
                             rr,
                             start=(ki == 0), stop=(ki == nk2 - 1)))
                    ki += 1
                if extra_rhs is not None:
                    w2_hbm, K2, x2, x2_c0, wbuf2 = extra_rhs
                    for kc in range(cdiv(K2, 128)):
                        krows = min(128, K2 - kc * 128)
                        p.op("tensor", lambda e, kc=kc, krows=krows, mc0=mc0,
                             mb0=mb0, t0=t0, tw=tw, psl=psl, mrows=mrows,
                             ki=ki: e.matmul(
                                 psl,
                                 wbuf2[0:krows, kc, mc0 - mb0: mc0 - mb0 + mrows],
                                 x2[0:krows, x2_c0 + kc, t0: t0 + tw],
                                 start=(ki == 0), stop=(ki == nk2 - 1)))
                        ki += 1

                def ev(e, mc=mc, mc0=mc0, t0=t0, tw=tw, psl=psl, mrows=mrows):
                    if dst_fn is not None:
                        d = dst_fn(mc, mrows, t0, tw)
                    else:
                        d = dst[0:mrows, dst_c0 + mc, dst_tok0 + t0: dst_tok0 + t0 + tw]
                    if accum:
                        b = bias[0:mrows, mc:mc + 1] if bias is not None else 0.0
                        return e.scalar_tensor_tensor(d, psl, b, d,
                                                      op0=ALU.add, op1=ALU.add)
                    if act is not None:
                        b = bias[0:mrows, mc:mc + 1] if bias is not None else 0.0
                        return e.activation(d, psl, act, bias=b, scale=act_scale)
                    if bias is not None:
                        return e.activation(d, psl, AF.Identity,
                                            bias=bias[0:mrows, mc:mc + 1], scale=1.0)
                    return e.copy(d, psl)
                p.op("vector" if accum else "scalar", ev)


def ln_rows(p, cb, src, Tn, src_fn=None):
    st = cb["stats"]
    s0 = st[0:1, 0:Tn]
    s1 = st[0:1, Tn:2 * Tn]
    sq = cb["sq"]
    ps_st = cb["ps_st"]

    def sl(kc, t0, tw):
        if src_fn is not None:
            return src_fn(kc, t0, tw)
        return src[:, kc, t0:t0 + tw]

    for t0 in range(0, Tn, 512):
        for kc in range(NCH):
            p.op("tensor", lambda e, kc=kc, t0=t0: e.matmul(
                ps_st, cb["ones1"][:, 0:1], sl(kc, t0, 512),
                start=(kc == 0), stop=(kc == NCH - 1)))
        p.op("vector", lambda e, t0=t0: e.tensor_copy(st[0:1, t0:t0 + 512], ps_st))
        for kc in range(NCH):
            p.op("vector", lambda e, kc=kc, t0=t0: e.tensor_mul(
                sq[:, 0:512], sl(kc, t0, 512), sl(kc, t0, 512)))
            p.op("tensor", lambda e, kc=kc: e.matmul(
                ps_st, cb["ones1"][:, 0:1], sq[:, 0:512],
                start=(kc == 0), stop=(kc == NCH - 1)))
        p.op("vector", lambda e, t0=t0: e.tensor_copy(st[0:1, Tn + t0:Tn + t0 + 512], ps_st))
    Dn = float(D)
    p.op("vector", lambda e: e.tensor_scalar_mul(s0, s0, -1.0 / Dn))
    p.op("vector", lambda e: e.tensor_scalar_mul(s1, s1, 1.0 / Dn))
    tmp = cb["strow"]
    p.op("vector", lambda e: e.tensor_mul(tmp[0:1, 0:Tn], s0, s0))
    p.op("vector", lambda e: e.tensor_sub(s1, s1, tmp[0:1, 0:Tn]))
    p.op("scalar", lambda e: e.activation(s1, s1, AF.Sqrt,
                                          bias=cb["epsc"][0:1, 0:1], scale=1.0))
    p.op("vector", lambda e: e.reciprocal(s1, s1))
    p.op("vector", lambda e: e.tensor_mul(s0, s0, s1))
    return s1, s0


def bcast_row(p, cb, src_row, dst128, Tn, dtype=BF16):
    rb = cb["rowbf"]
    p.op("vector", lambda e: e.tensor_copy(rb[0:1, 0:Tn], src_row))
    for t0 in range(0, Tn, 512):
        p.op("tensor", lambda e, t0=t0: e.matmul(
            cb["ps_bc"], cb["ones1"][0:1, 0:128], rb[0:1, t0:t0 + 512],
            start=True, stop=True))
        p.op("scalar", lambda e, t0=t0: e.copy(dst128[:, t0:t0 + 512], cb["ps_bc"]))


def alloc_common(p, ps):
    cb = {}
    cb["ones1"] = p.sbuf("ones1", [128, 128], BF16)
    cb["stats"] = p.sbuf("stats", [1, 2 * T], F32)
    cb["strow"] = p.sbuf("strow", [1, T], F32)
    cb["rowbf"] = p.sbuf("rowbf", [1, T], BF16)
    cb["sq"] = p.sbuf("sq", [128, 512], BF16)
    cb["A128"] = p.sbuf("A128", [128, T], BF16)
    cb["B128"] = p.sbuf("B128", [128, T], BF16)
    cb["epsc"] = p.sbuf("epsc", [128, 1], F32)
    cb["ps_st"] = ps[0:1, 3072:3584]
    cb["ps_bc"] = ps[0:128, 3072:3584]
    p.op("vector", lambda e: e.memset(cb["ones1"][:, :], 1.0))
    p.op("vector", lambda e: e.memset(cb["epsc"][:, :], EPS))
    return cb


def build_half(body, extra_decl, name="half", Tn=T, alloc_f=True):
    p = P(name)
    x_in = p.hbm_in("x_in", [D, Tn])
    x_out = p.hbm_out("x_out", [D, Tn])
    extra_decl(p)

    ps = p.psum("ps", [128, 3584])
    cb = alloc_common(p, ps)
    xnb = p.sbuf("xn", [128, NCH * (Tn + 1)], BF16)
    xn = V3(xnb, NCH, Tn, pitch=Tn + 1, off=1)
    if alloc_f:
        fb = p.sbuf("fbuf", [128, NCH * Tn], BF16)
        fv = V3(fb, NCH, Tn)
    else:
        fv = None

    p.op("vector", lambda e: e.memset(
        bass.AP(xnb, 0, [[NCH * (Tn + 1), 128], [Tn + 1, NCH], [1, 1]]), 0.0))
    xv = x_in.rearrange("(c p) t -> p c t", p=128)
    p.dma("gpsimd", xn[:, 0:NCH, 0:Tn], xv)
    A, B = ln_rows(p, cb, xn, Tn)
    bcast_row(p, cb, A, cb["A128"], Tn)
    bcast_row(p, cb, B, cb["B128"], Tn)
    Av = bview(cb["A128"], NCH, Tn)
    Bv = bview(cb["B128"], NCH, Tn)
    p.op("vector", lambda e: e.tensor_mul(xn[:, 0:NCH, 0:Tn], xn[:, 0:NCH, 0:Tn], Av))
    p.op("vector", lambda e: e.tensor_add(xn[:, 0:NCH, 0:Tn], xn[:, 0:NCH, 0:Tn], Bv))

    ret = body(p, cb, xn, fv, ps)
    if ret is not None:
        fv = ret

    A2, B2 = ln_rows(p, cb, fv, Tn)
    bcast_row(p, cb, A2, cb["A128"], Tn)
    bcast_row(p, cb, B2, cb["B128"], Tn)
    SW = 32
    xs = p.sbuf("xstream", [128, NCH, SW], F32)
    ts = p.sbuf("tstream", [128, NCH, SW], BF16)
    xov = x_out.rearrange("(c p) t -> p c t", p=128)
    for t0 in range(0, Tn, SW):
        p.dma("sync", xs[:, :, :], xv[:, :, t0:t0 + SW])
        p.op("vector", lambda e, t0=t0: e.tensor_mul(
            ts[:, 0:NCH, 0:SW], fv[:, 0:NCH, t0:t0 + SW],
            bass.AP(cb["A128"], t0, [[Tn, 128], [0, NCH], [1, SW]])))
        p.op("vector", lambda e, t0=t0: e.tensor_add(
            ts[:, 0:NCH, 0:SW], ts[:, 0:NCH, 0:SW],
            bass.AP(cb["B128"], t0, [[Tn, 128], [0, NCH], [1, SW]])))
        p.op("vector", lambda e, t0=t0: e.tensor_add(
            xs[:, 0:NCH, 0:SW], xs[:, 0:NCH, 0:SW], ts[:, 0:NCH, 0:SW]))
        p.dma("sync", xov[:, :, t0:t0 + SW], xs[:, :, :])
    return p.finish()


def decl_l3_attn(p):
    for w in ("wq", "wk", "wv", "wo"):
        p.hbm_in(w, [D, D])
    for b in ("bq", "bk", "bv", "bo"):
        p.hbm_in(b, [D])
    p.hbm_in("sel", [16, NCH * 128])
    p.hbm_in("ident", [128, 128])


def body_l3_attn(p, cb, xn, fv, ps):
    wbuf = p.sbuf("wbuf", [128, NCH, 512], BF16)
    qb = fv.t
    qv = fv
    kb = p.sbuf("kb", [128, NCH * T], BF16); kv = V3(kb, NCH, T)
    ksw = p.sbuf("ksw", [128, NCH * T], BF16); kwv = V3(ksw, NCH, T)
    expb = p.sbuf("expb", [128, 2048], F32)
    den = p.sbuf("den", [128, 16], F32)
    dia = p.sbuf("dia", [128, 16], F32)
    fac = p.sbuf("fac", [128, 16], BF16)
    facT = p.sbuf("facT", [16, 16 * 128], BF16)
    fac128 = p.sbuf("fac128", [128, 128], BF16)
    selb = p.sbuf("selb", [16, NCH * 128], BF16)
    idb = p.sbuf("idb", [128, 128], BF16)
    p.dma("gpsimd", selb[:, :], p.t["sel"])
    p.dma("gpsimd", idb[:, :], p.t["ident"])
    psT = p.psum("psT", [16, 128], BF16)

    bq = load_col(p, "bqc", p.t["bq"], D)
    bk = load_col(p, "bkc", p.t["bk"], D)
    bv_ = load_col(p, "bvc", p.t["bv"], D)
    bo = load_col(p, "boc", p.t["bo"], D)

    linear(p, "q", p.t["wq"], D, D, xn, qv, T, wbuf, ps[:, 0:512], bias=bq)
    linear(p, "k", p.t["wk"], D, D, xn, kv, T, wbuf, ps[:, 0:512], bias=bk)
    p.dma("sync", ksw[0:64, 0:NCH * T], kb[64:128, 0:NCH * T])
    p.dma("sync", ksw[64:128, 0:NCH * T], kb[0:64, 0:NCH * T])

    SC = 1.0 / 1024.0
    for h in range(16):
        t0 = h * 128
        for gb in range(16):
            for ai in range(4):
                par, ch0 = ai % 2, (ai // 2) * 4
                ksrc = kb if (gb % 2) == par else ksw
                kl = bass.AP(ksrc, (64 * par) * (NCH * T) + (gb // 2) * T + t0,
                             [[NCH * T, 64], [1, 128]])
                rhs = bass.AP(qb, (64 * par) * (NCH * T) + ch0 * T + t0,
                              [[NCH * T, 64], [T, 4], [1, 128]])
                p.op("tensor", lambda e, kl=kl, rhs=rhs, ai=ai: e.matmul(
                    ps[0:128, 512 * ai:512 * ai + 512], kl, rhs,
                    start=True, stop=True))
            p.op("scalar", lambda e: e.activation(
                expb[:, 0:2048], ps[0:128, 0:2048], AF.Exp, bias=0.0, scale=SC))
            p.op("vector", lambda e, gb=gb: e.tensor_reduce(
                den[:, gb:gb + 1], bass.AP(expb, 0, [[2048, 128], [512, 4], [1, 512]]),
                axis=AX.XY, op=ALU.add))
            ai_d = (gb % 2) + 2 * ((gb // 2) // 4)
            col = ai_d * 512 + ((gb // 2) % 4) * 128
            p.op("vector", lambda e, col=col: e.tensor_mul(
                expb[:, 0:128], expb[:, col:col + 128], idb[:, :]))
            p.op("vector", lambda e, gb=gb: e.tensor_reduce(
                dia[:, gb:gb + 1], expb[:, 0:128], axis=AX.X, op=ALU.add))
        p.op("vector", lambda e: e.reciprocal(den[:, 0:16], den[:, 0:16]))
        p.op("vector", lambda e: e.tensor_mul(den[:, 0:16], den[:, 0:16], dia[:, 0:16]))
        p.op("vector", lambda e: e.tensor_copy(fac[:, 0:16], den[:, 0:16]))
        p.op("tensor", lambda e: e.transpose(psT[0:16, 0:128], fac[:, 0:16], idb[:, :]))
        p.op("vector", lambda e, h=h: e.tensor_copy(facT[:, h * 128:h * 128 + 128],
                                                    psT[0:16, 0:128]))
    vv = kv
    linear(p, "v", p.t["wv"], D, D, xn, vv, T, wbuf, ps[:, 0:512], bias=bv_)
    for h in range(16):
        t0 = h * 128
        for c in range(NCH):
            p.op("tensor", lambda e, c=c, h=h: e.matmul(
                ps[0:128, 0:128], selb[0:16, c * 128:c * 128 + 128],
                facT[0:16, h * 128:h * 128 + 128], start=True, stop=True))
            p.op("scalar", lambda e: e.copy(fac128[:, :], ps[0:128, 0:128]))
            p.op("vector", lambda e, c=c, t0=t0: e.tensor_mul(
                vv[:, c, t0:t0 + 128], vv[:, c, t0:t0 + 128], fac128[:, :]))
    linear(p, "o", p.t["wo"], D, D, vv, fv, T, wbuf, ps[:, 0:512], bias=bo)


def decl_l1_ffn(p):
    for w in ("wf", "uf", "wi", "ui", "wc", "uc", "wo", "uo"):
        p.hbm_in(w, [D, D])
    for b in ("bf", "bi", "bc", "bo"):
        p.hbm_in(b, [D])
    p.hbm_in("in_c", [D])


def body_l1_ffn(p, cb, xn, fv, ps):
    NB = 256
    wbuf = p.sbuf("wbuf", [128, NCH, 512], BF16)
    wbuf2 = p.sbuf("wbuf2", [128, NCH, 512], BF16)
    cf = p.sbuf("cf", [128, NCH * NB], F32); cv = V3(cf, NCH, NB)
    cbf = p.sbuf("cbf", [128, NCH * NB], BF16); cbv = V3(cbf, NCH, NB)
    t1 = p.sbuf("t1", [128, NCH * NB], F32); t1v = V3(t1, NCH, NB)
    t2 = p.sbuf("t2", [128, NCH * NB], F32); t2v = V3(t2, NCH, NB)
    ucon = p.sbuf("ucon", [128, NCH * NB], F32); uv = V3(ucon, NCH, NB)
    bfc = load_col(p, "bfc", p.t["bf"], D)
    bic = load_col(p, "bic", p.t["bi"], D)
    bcc = load_col(p, "bcc", p.t["bc"], D)
    boc = load_col(p, "boc", p.t["bo"], D)
    icc = load_col(p, "icc", p.t["in_c"], D)

    p.op("vector", lambda e: e.memset(cf[:, :], 0.0))
    for ch in range(NCH):
        p.op("vector", lambda e, ch=ch: e.tensor_scalar(
            cv[:, ch, 0:NB], cv[:, ch, 0:NB], icc[:, ch:ch + 1], None, op0=ALU.add))
    p.op("vector", lambda e: e.tensor_copy(cbf[:, :], cf[:, :]))

    for step in range(8):
        tok0 = step * NB
        for wnm, unm, bc_, dstv, fn in (
                ("wi", "ui", bic, t1v, AF.Sigmoid),
                ("wc", "uc", bcc, t2v, AF.Tanh)):
            linear(p, "g", p.t[wnm], D, D, xn, dstv, NB, wbuf, ps[:, 0:512],
                   bias=bc_, act=fn, tok0=tok0, dst_tok0=0, n_tile=NB,
                   extra_rhs=(p.t[unm], D, cbv, 0, wbuf2))
        p.op("vector", lambda e: e.tensor_mul(t1v[:, 0:NCH, 0:NB],
                                              t1v[:, 0:NCH, 0:NB], t2v[:, 0:NCH, 0:NB]))
        linear(p, "g", p.t["wf"], D, D, xn, t2v, NB, wbuf, ps[:, 0:512],
               bias=bfc, act=AF.Sigmoid, tok0=tok0, dst_tok0=0, n_tile=NB,
               extra_rhs=(p.t["uf"], D, cbv, 0, wbuf2))
        p.op("vector", lambda e: e.tensor_mul(cv[:, 0:NCH, 0:NB],
                                              cv[:, 0:NCH, 0:NB], t2v[:, 0:NCH, 0:NB]))
        p.op("vector", lambda e: e.tensor_add(cv[:, 0:NCH, 0:NB],
                                              cv[:, 0:NCH, 0:NB], t1v[:, 0:NCH, 0:NB]))
        p.op("vector", lambda e: e.tensor_copy(cbf[:, :], cf[:, :]))

    linear(p, "uo", p.t["uo"], D, D, cbv, uv, NB, wbuf, ps[:, 0:512])
    nkc = NCH
    for mb0 in range(0, D, 512):
        load_w(p, None, p.t["wo"], D, D, m0=mb0, m1=mb0 + 512, sb=wbuf)
        for mc0 in range(mb0, mb0 + 512, 128):
            mc = mc0 // 128
            for t0 in range(0, T, 512):
                psl = ps[0:128, 0:512]
                for kc in range(nkc):
                    p.op("tensor", lambda e, kc=kc, mc0=mc0, mb0=mb0, t0=t0, psl=psl: e.matmul(
                        psl, wbuf[:, kc, mc0 - mb0: mc0 - mb0 + 128],
                        xn[:, kc, t0: t0 + 512],
                        start=(kc == 0), stop=(kc == nkc - 1)))
                p.op("vector", lambda e, mc=mc, t0=t0, psl=psl: e.tensor_add(
                    psl, psl, bass.AP(ucon, mc * 256, [[NCH * 256, 128], [0, 2], [1, 256]])))
                p.op("scalar", lambda e, mc=mc, t0=t0, psl=psl: e.activation(
                    fv[:, mc, t0:t0 + 512], psl, AF.Tanh,
                    bias=boc[:, mc:mc + 1], scale=1.0))


def decl_l1_attn(p):
    p.hbm_in("lng", [D])
    p.hbm_in("lnb", [D])
    p.hbm_in("wq1", [D, D])
    p.hbm_in("wk1", [D, D])
    p.hbm_in("wv1", [D, D])
    p.hbm_in("wout", [D, D])
    p.hbm_in("bandmask", [128, 256])
    p.hbm_in("ident", [128, 128])


def body_l1_attn(p, cb, xn, fv, ps):
    wbuf = p.sbuf("wbuf", [128, NCH, 512], BF16)
    kb = p.sbuf("kb", [128, NCH * T], BF16); kv = V3(kb, NCH, T)
    vtb = p.sbuf("vtb", [128, 16, 1024], BF16)
    expb = p.sbuf("expb", [128, 256], F32)
    attb = p.sbuf("attb", [128, 256], BF16)
    attT = p.sbuf("attT", [128, 256], BF16)
    denc = p.sbuf("denc", [128, 1], F32)
    bmask = p.sbuf("bmask", [128, 256], F32)
    idb = p.sbuf("idb", [128, 128], BF16)
    p.dma("sync", bmask[:, :], p.t["bandmask"])
    p.dma("gpsimd", idb[:, :], p.t["ident"])
    psbf = p.psum("psbf", [128, 256], BF16)
    gc = load_col(p, "gc", p.t["lng"], D)
    bc = load_col(p, "bc", p.t["lnb"], D)
    qv = fv

    for ch in range(NCH):
        p.op("scalar", lambda e, ch=ch: e.activation(
            xn[:, ch, 0:T], xn[:, ch, 0:T], AF.Identity,
            bias=bc[:, ch:ch + 1], scale=gc[:, ch:ch + 1]))

    wq, wk, wv = p.t["wq1"], p.t["wk1"], p.t["wv1"]
    linear(p, "q", wq, D, D, xn, qv, T, wbuf, ps[:, 0:512])
    linear(p, "k", wk, D, D, xn, kv, T, wbuf, ps[:, 0:512])
    for mb0 in range(0, D, 512):
        load_w(p, None, wv, D, D, m0=mb0, m1=mb0 + 512, sb=wbuf)
        for tt in range(16):
            psl = ps[0:128, 0:512]
            for kc in range(NCH):
                p.op("tensor", lambda e, kc=kc, tt=tt, psl=psl: e.matmul(
                    psl, xn[:, kc, tt * 128:tt * 128 + 128],
                    wbuf[:, kc, 0:512],
                    start=(kc == 0), stop=(kc == NCH - 1)))
            p.op("scalar", lambda e, tt=tt, mb0=mb0, psl=psl: e.copy(
                vtb[:, tt, mb0:mb0 + 512], psl))

    for qt in range(16):
        for h in range(16):
            par = h % 2
            ch = h // 2
            pb = 64 * par
            ql = bass.AP(qv.t, pb * (NCH * T) + ch * T + qt * 128,
                         [[NCH * T, 64], [1, 128]])
            j0 = 128 if qt == 0 else 0
            kcols = 128 if qt == 0 else 256
            kl = bass.AP(kb, pb * (NCH * T) + ch * T + max(qt - 1, 0) * 128,
                         [[NCH * T, 64], [1, kcols]])
            p.op("tensor", lambda e, ql=ql, kl=kl, kcols=kcols: e.matmul(
                ps[0:128, 0:kcols], ql, kl, start=True, stop=True))
            p.op("scalar", lambda e, kcols=kcols: e.activation(
                expb[:, 0:kcols], ps[0:128, 0:kcols], AF.Exp, bias=0.0, scale=0.125))
            p.op("vector", lambda e, j0=j0, kcols=kcols: e.tensor_mul(
                expb[:, 0:kcols], expb[:, 0:kcols], bmask[:, j0:j0 + kcols]))
            p.op("vector", lambda e, kcols=kcols: e.tensor_reduce(
                denc[:, 0:1], expb[:, 0:kcols], axis=AX.X, op=ALU.add))
            p.op("vector", lambda e: e.reciprocal(denc[:, 0:1], denc[:, 0:1]))
            p.op("scalar", lambda e, kcols=kcols: e.activation(
                attb[:, 0:kcols], expb[:, 0:kcols], AF.Identity,
                bias=0.0, scale=denc[:, 0:1]))
            nkt = kcols // 128
            for jt in range(nkt):
                p.op("tensor", lambda e, jt=jt: e.transpose(
                    psbf[0:128, jt * 128:jt * 128 + 128],
                    attb[:, jt * 128:jt * 128 + 128], idb[:, :]))
            p.op("vector", lambda e, kcols=kcols: e.tensor_copy(
                attT[:, 0:kcols], psbf[0:128, 0:kcols]))
            for jt in range(nkt):
                kt = max(qt - 1, 0) + jt
                p.op("tensor", lambda e, jt=jt, kt=kt, h=h, pb=pb, nkt=nkt: e.matmul(
                    ps[pb:pb + 64, 512:640],
                    vtb[:, kt, h * 64:h * 64 + 64],
                    attT[:, jt * 128:jt * 128 + 128],
                    start=(jt == 0), stop=(jt == nkt - 1)))
            p.op("scalar", lambda e, h=h, ch=ch, pb=pb, qt=qt: e.copy(
                bass.AP(qv.t, pb * (NCH * T) + ch * T + qt * 128,
                        [[NCH * T, 64], [1, 128]]),
                ps[pb:pb + 64, 512:640]))

    if DBG:
        p.dump("dbg_o", qv[:, 0:NCH, 0:T], [128, NCH * T], BF16)
        p.dump("dbg_k", kv[:, 0:NCH, 0:T], [128, NCH * T], BF16)
        p.dump("dbg_v", vtb[:, :, :], [128, 16 * 1024], BF16)
    fo = V3(kb, NCH, T)
    linear(p, "o", p.t["wout"], D, D, qv, fo, T, wbuf, ps[:, 0:512])
    return fo


HID = 2730
NSLAB = cdiv(HID, 128)


def decl_ffn(p):
    for l in range(2):
        p.hbm_in(f"wq{l}", [D, 512])
        p.hbm_in(f"wk{l}", [D, 512])
        p.hbm_in(f"wv{l}", [D, 512])
        p.hbm_in(f"wo{l}", [512, D])
        p.hbm_in(f"w1p{l}", [D, 2 * HID])
        p.hbm_in(f"w2{l}", [HID, D])
        p.hbm_in(f"qs{l}", [128])
        p.hbm_in(f"ks{l}", [128])
        p.hbm_in(f"lng{l}", [D])
        p.hbm_in(f"lnb{l}", [D])
        p.hbm_in(f"b1a{l}", [HID])
        p.hbm_in(f"b1g{l}", [HID])
        p.hbm_in(f"b2{l}", [D])
    p.hbm_in("rotm", [128, 128])
    p.hbm_in("csq", [128, T])
    p.hbm_in("ssq", [128, T])
    p.hbm_in("csk", [128, T])
    p.hbm_in("ssk", [128, T])
    p.hbm_in("bandmask", [128, 256])
    p.hbm_in("ident", [128, 128])
    p.hbm_in("bexph", [2, 128])


def body_ffn(p, cb, xn, fv_none, ps):
    arena = p.sbuf("arena", [128, 24576], BF16)
    qa = V3(arena, 4, T)
    ka = V3(arena, 4, T, base=4 * T)
    vt = V3(arena, 16, 512, base=8 * T)
    AG = V3(arena, NSLAB, 1024)
    xrb = p.sbuf("xr", [128, NCH * (T + 1)], BF16)
    xr = V3(xrb, NCH, T, pitch=T + 1, off=1)
    xrs = V3(xrb, NCH, T, pitch=T + 1, off=0)
    xq = V3(xn.t, NCH, T, pitch=T + 1, off=1)
    xns = V3(xn.t, NCH, T, pitch=T + 1, off=0)
    wbig = p.sbuf("wbig", [128, NSLAB, 256], BF16)
    tbl = p.sbuf("tbl", [128, 4, 512], BF16)
    rotb = p.sbuf("rotb", [128, 512], BF16)
    rotm = p.sbuf("rotm_s", [128, 128], BF16)
    idb = p.sbuf("idb", [128, 128], BF16)
    bmask = p.sbuf("bmask", [128, 256], F32)
    bones = p.sbuf("bones", [128, 2], BF16)
    bexp = p.sbuf("bexp", [2, 128], BF16)
    p.dma("gpsimd", bexp[:, :], p.t["bexph"])
    nrm = p.sbuf("nrm", [2, 512], F32)
    expb = p.sbuf("expb", [128, 640], F32)
    attb = p.sbuf("attb", [128, 640], BF16)
    attT = p.sbuf("attT", [128, 640], BF16)
    denc = p.sbuf("denc", [128, 1], F32)
    gtmp = p.sbuf("gtmp", [128, 512], BF16)
    psbf = p.psum("psbf", [128, 640], BF16)
    p.dma("gpsimd", rotm[:, :], p.t["rotm"])
    p.dma("gpsimd", idb[:, :], p.t["ident"])
    p.dma("sync", bmask[:, :], p.t["bandmask"])
    p.op("vector", lambda e: e.memset(bones[:, :], 0.0))
    p.op("vector", lambda e: e.memset(bones[0:64, 0:1], 1.0))
    p.op("vector", lambda e: e.memset(bones[64:128, 1:2], 1.0))
    p.op("vector", lambda e: e.memset(
        bass.AP(xrb, 0, [[NCH * (T + 1), 128], [T + 1, NCH], [1, 1]]), 0.0))

    cols = {}
    for l in range(2):
        cols[f"qs{l}"] = load_col(p, f"qsc{l}", p.t[f"qs{l}"], 128)
        cols[f"ks{l}"] = load_col(p, f"ksc{l}", p.t[f"ks{l}"], 128)
        cols[f"lng{l}"] = load_col(p, f"lngc{l}", p.t[f"lng{l}"], D)
        cols[f"lnb{l}"] = load_col(p, f"lnbc{l}", p.t[f"lnb{l}"], D)
        cols[f"b1a{l}"] = load_col(p, f"b1ac{l}", p.t[f"b1a{l}"], HID)
        cols[f"b1g{l}"] = load_col(p, f"b1gc{l}", p.t[f"b1g{l}"], HID)
        cols[f"b2{l}"] = load_col(p, f"b2c{l}", p.t[f"b2{l}"], D)

    for l in range(2):
        xu = xn if l == 0 else xr
        xs = xns if l == 0 else xrs

        def shrhs(kc, t0, tw, xu=xu, xs=xs):
            v = xu if kc < 4 else xs
            return v[:, kc, t0:t0 + tw]

        linear(p, "q", p.t[f"wq{l}"], D, 512, xu, qa, T, wbig, ps[:, 0:512],
               rhs_fn=shrhs, mblk=256)
        linear(p, "k", p.t[f"wk{l}"], D, 512, xu, ka, T, wbig, ps[:, 0:512],
               rhs_fn=shrhs, mblk=256)
        for mb0 in range(0, 512, 256):
            load_w(p, None, p.t[f"wv{l}"], D, 512, m0=mb0, m1=mb0 + 256, sb=wbig)
            for tt in range(16):
                psl = ps[0:128, 0:256]
                for kc in range(NCH):
                    vsrc = xu if kc < 4 else xs
                    p.op("tensor", lambda e, kc=kc, tt=tt, psl=psl, vsrc=vsrc: e.matmul(
                        psl, vsrc[:, kc, tt * 128:tt * 128 + 128],
                        wbig[:, kc, 0:256],
                        start=(kc == 0), stop=(kc == NCH - 1)))
                p.op("scalar", lambda e, tt=tt, mb0=mb0, psl=psl: e.copy(
                    vt[:, tt, mb0:mb0 + 256], psl))
        if l == 0:
            p.op("vector", lambda e: e.tensor_copy(xr[:, 0:NCH, 0:T],
                                                   xn[:, 0:NCH, 0:T]))

        if DBG and l == 0:
            p.dump("dbg_q0", qa[:, 0:4, 0:T], [128, 4 * T], BF16)
            p.dump("dbg_k0", ka[:, 0:4, 0:T], [128, 4 * T], BF16)
            p.dump("dbg_v0", vt[:, 0:16, 0:512], [128, 16 * 512], BF16)
        for which, buf, scol, ct, st_ in (
                (0, qa, cols[f"qs{l}"], "csq", "ssq"),
                (1, ka, cols[f"ks{l}"], "csk", "ssk")):
            for tt in range(4):
                t0 = tt * 512
                p.dma("gpsimd", tbl[:, 0, 0:512],
                      bass.AP(p.t[ct].tensor, p.t[ct].offset + t0, [[T, 128], [1, 512]]))
                p.dma("gpsimd", tbl[:, 1, 0:512],
                      bass.AP(p.t[st_].tensor, p.t[st_].offset + t0, [[T, 128], [1, 512]]))
                for ch in range(4):
                    p.op("vector", lambda e, ch=ch, t0=t0, buf=buf: e.tensor_mul(
                        gtmp[:, 0:512], buf[:, ch, t0:t0 + 512], buf[:, ch, t0:t0 + 512]))
                    p.op("tensor", lambda e: e.matmul(
                        ps[0:2, 1024:1536], bones[:, 0:2], gtmp[:, 0:512],
                        start=True, stop=True))
                    p.op("scalar", lambda e: e.activation(
                        nrm[0:2, 0:512], ps[0:2, 1024:1536], AF.Sqrt,
                        bias=0.0, scale=1.0))
                    p.op("vector", lambda e: e.tensor_single_scalar(
                        nrm[0:2, 0:512], nrm[0:2, 0:512], 1e-12, op=ALU.max))
                    p.op("vector", lambda e: e.reciprocal(nrm[0:2, 0:512], nrm[0:2, 0:512]))
                    p.op("vector", lambda e: e.tensor_copy(gtmp[0:2, 0:512], nrm[0:2, 0:512]))
                    p.op("tensor", lambda e: e.matmul(
                        ps[0:128, 1024:1536], bexp[0:2, 0:128], gtmp[0:2, 0:512],
                        start=True, stop=True))
                    p.op("vector", lambda e, ch=ch, t0=t0, buf=buf: e.tensor_mul(
                        buf[:, ch, t0:t0 + 512], buf[:, ch, t0:t0 + 512],
                        ps[0:128, 1024:1536]))
                    p.op("scalar", lambda e, ch=ch, t0=t0, buf=buf, scol=scol: e.activation(
                        buf[:, ch, t0:t0 + 512], buf[:, ch, t0:t0 + 512], AF.Identity,
                        bias=0.0, scale=scol[:, 0:1]))
                    p.op("tensor", lambda e, ch=ch, t0=t0, buf=buf: e.matmul(
                        ps[0:128, 1536:2048], rotm[:, :], buf[:, ch, t0:t0 + 512],
                        start=True, stop=True))
                    p.op("vector", lambda e: e.tensor_copy(rotb[:, 0:512],
                                                           ps[0:128, 1536:2048]))
                    p.op("vector", lambda e, ch=ch, t0=t0, buf=buf: e.tensor_mul(
                        buf[:, ch, t0:t0 + 512], buf[:, ch, t0:t0 + 512],
                        tbl[:, 0, 0:512]))
                    p.op("vector", lambda e: e.tensor_mul(
                        rotb[:, 0:512], rotb[:, 0:512], tbl[:, 1, 0:512]))
                    p.op("vector", lambda e, ch=ch, t0=t0, buf=buf: e.tensor_add(
                        buf[:, ch, t0:t0 + 512], buf[:, ch, t0:t0 + 512],
                        rotb[:, 0:512]))

        if DBG and l == 0:
            p.dump("dbg_q1", qa[:, 0:4, 0:T], [128, 4 * T], BF16)
            p.dump("dbg_k1", ka[:, 0:4, 0:T], [128, 4 * T], BF16)
        for qt in range(16):
            nk = min(qt, 4) + 1
            k0 = (qt - nk + 1) * 128
            kcols = nk * 128
            for h in range(8):
                par, ch = h % 2, h // 2
                pb = 64 * par
                ql = bass.AP(arena, pb * T * 12 + ch * T + qt * 128,
                             [[12 * T, 64], [1, 128]])
                n1 = min(kcols, 512)
                p.op("tensor", lambda e, ql=ql, pb=pb, ch=ch, k0=k0, n1=n1: e.matmul(
                    ps[0:128, 0:n1], ql,
                    bass.AP(arena, pb * T * 12 + 4 * T + ch * T + k0, [[12 * T, 64], [1, n1]]),
                    start=True, stop=True))
                if kcols > 512:
                    p.op("tensor", lambda e, ql=ql, pb=pb, ch=ch, k0=k0, kcols=kcols: e.matmul(
                        ps[0:128, 512:kcols], ql,
                        bass.AP(arena, pb * T * 12 + 4 * T + ch * T + k0 + 512,
                                [[12 * T, 64], [1, kcols - 512]]),
                        start=True, stop=True))
                p.op("scalar", lambda e, kcols=kcols: e.activation(
                    expb[:, 0:kcols], ps[0:128, 0:kcols], AF.Exp, bias=0.0, scale=8.0))
                if nk == 5:
                    p.op("vector", lambda e: e.tensor_mul(
                        expb[:, 0:128], expb[:, 0:128], bmask[:, 0:128]))
                p.op("vector", lambda e, kcols=kcols: e.tensor_mul(
                    expb[:, kcols - 128:kcols], expb[:, kcols - 128:kcols],
                    bmask[:, 128:256]))
                p.op("vector", lambda e, kcols=kcols: e.tensor_reduce(
                    denc[:, 0:1], expb[:, 0:kcols], axis=AX.X, op=ALU.add))
                p.op("vector", lambda e: e.reciprocal(denc[:, 0:1], denc[:, 0:1]))
                p.op("scalar", lambda e, kcols=kcols: e.activation(
                    attb[:, 0:kcols], expb[:, 0:kcols], AF.Identity,
                    bias=0.0, scale=denc[:, 0:1]))
                for jt in range(nk):
                    p.op("tensor", lambda e, jt=jt: e.transpose(
                        psbf[0:128, jt * 128:jt * 128 + 128],
                        attb[:, jt * 128:jt * 128 + 128], idb[:, :]))
                p.op("vector", lambda e, kcols=kcols: e.tensor_copy(
                    attT[:, 0:kcols], psbf[0:128, 0:kcols]))
                for jt in range(nk):
                    kt = (k0 // 128) + jt
                    p.op("tensor", lambda e, jt=jt, kt=kt, h=h, pb=pb, nk=nk: e.matmul(
                        ps[pb:pb + 64, 1024:1152],
                        vt[0:128, kt, h * 64:h * 64 + 64],
                        attT[:, jt * 128:jt * 128 + 128],
                        start=(jt == 0), stop=(jt == nk - 1)))
                p.op("scalar", lambda e, pb=pb, ch=ch, qt=qt: e.copy(
                    bass.AP(arena, pb * T * 12 + ch * T + qt * 128, [[12 * T, 64], [1, 128]]),
                    ps[pb:pb + 64, 1024:1152]))

        if DBG and l == 0:
            p.dump("dbg_o", qa[:, 0:4, 0:T], [128, 4 * T], BF16)
        linear(p, "wo", p.t[f"wo{l}"], 512, D, qa, xr, T, wbig, ps[:, 0:512],
               accum=True, mblk=256)
        if DBG and l == 0:
            p.dump("dbg_xr1", xr[:, 0:NCH, 0:T], [128, NCH * T], BF16)

        def ffnsrc(kc, t0, tw):
            v = xr if kc < 4 else xrs
            return v[:, kc, t0:t0 + tw]

        A, B = ln_rows(p, cb, None, T, src_fn=ffnsrc)
        bcast_row(p, cb, A, cb["A128"], T)
        bcast_row(p, cb, B, cb["B128"], T)
        if DBG and l == 0:
            p.dump("dbg_A", cb["A128"][:, 0:T], [128, T], BF16)
            p.dump("dbg_B", cb["B128"][:, 0:T], [128, T], BF16)
        for kc in range(NCH):
            p.op("vector", lambda e, kc=kc: e.tensor_mul(
                xq[:, kc, 0:T], ffnsrc(kc, 0, T), cb["A128"][:, 0:T]))
            p.op("vector", lambda e, kc=kc: e.tensor_add(
                xq[:, kc, 0:T], xq[:, kc, 0:T], cb["B128"][:, 0:T]))
            p.op("scalar", lambda e, kc=kc, bc_=cols[f"lnb{l}"], gc_=cols[f"lng{l}"]:
                 e.activation(
                xq[:, kc, 0:T], xq[:, kc, 0:T], AF.Identity,
                bias=bc_[:, kc:kc + 1], scale=gc_[:, kc:kc + 1]))

        for half in range(2):
            h0 = half * 1024
            coff = 0
            for slab in range(NSLAB):
                aw = min(128, HID - slab * 128)
                load_w(p, None, p.t[f"w1p{l}"], D, 2 * HID, m0=coff, m1=coff + 2 * aw,
                       sb=wbig)
                for tb in range(2):
                    t0 = h0 + tb * 512
                    for part in range(2):
                        for kc in range(NCH):
                            p.op("tensor", lambda e, kc=kc, part=part, aw=aw, t0=t0: e.matmul(
                                ps[0:aw, part * 512:part * 512 + 512],
                                wbig[:, kc, part * aw:part * aw + aw],
                                xq[:, kc, t0:t0 + 512],
                                start=(kc == 0), stop=(kc == NCH - 1)))
                    p.op("scalar", lambda e, aw=aw, slab=slab, bg_=cols[f"b1g{l}"]:
                         e.activation(
                        gtmp[0:aw, 0:512], ps[0:aw, 512:1024], AF.Gelu,
                        bias=bg_[0:aw, slab:slab + 1], scale=1.0))
                    p.op("vector", lambda e, aw=aw, slab=slab, tb=tb,
                         ba_=cols[f"b1a{l}"]: e.scalar_tensor_tensor(
                        AG[0:aw, slab, tb * 512:tb * 512 + 512],
                        ps[0:aw, 0:512], ba_[0:aw, slab:slab + 1],
                        gtmp[0:aw, 0:512], op0=ALU.add, op1=ALU.mult))
                coff += 2 * aw
            if DBG and l == 0 and half == 0:
                p.dump("dbg_ag", AG[0:128, 0:NSLAB, 0:1024], [128, NSLAB * 1024], BF16)
                p.dump("dbg_xq", xq[:, 0:NCH, 0:T], [128, NCH * T], BF16)
            for mb0 in range(0, D, 256):
                load_w(p, None, p.t[f"w2{l}"], HID, D, m0=mb0, m1=mb0 + 256, sb=wbig)
                for mc0 in range(mb0, mb0 + 256, 128):
                    for tb in range(2):
                        t0 = h0 + tb * 512
                        psl = ps[0:128, 0:512]
                        for kc in range(NSLAB):
                            krows = min(128, HID - kc * 128)
                            p.op("tensor", lambda e, kc=kc, krows=krows, mc0=mc0,
                                 mb0=mb0, tb=tb, psl=psl: e.matmul(
                                     psl, wbig[0:krows, kc, mc0 - mb0:mc0 - mb0 + 128],
                                     AG[0:krows, kc, tb * 512:tb * 512 + 512],
                                     start=(kc == 0), stop=(kc == NSLAB - 1)))
                        mc = mc0 // 128
                        p.op("vector", lambda e, mc=mc, t0=t0, psl=psl,
                             b2_=cols[f"b2{l}"]: e.scalar_tensor_tensor(
                            xr[:, mc, t0:t0 + 512], psl,
                            b2_[:, mc:mc + 1], xr[:, mc, t0:t0 + 512],
                            op0=ALU.add, op1=ALU.add))
    return xr


def decl_l3_ffn(p):
    p.hbm_in("iw", [D]); p.hbm_in("mu", [D])
    p.hbm_in("rw1", [D, 256]); p.hbm_in("rb1", [256])
    p.hbm_in("rw2", [256, 8]); p.hbm_in("rb2", [1, 8])
    p.hbm_in("hw1", [D, 256]); p.hbm_in("hb1", [256])
    p.hbm_in("hw2", [256, 1]); p.hbm_in("hb2", [1])
    p.hbm_in("w1all", [2 * D, 512]); p.hbm_in("b1all", [512])
    p.hbm_in("w2int", [512, 3072])
    p.hbm_in("b2eint", [8, 3072])
    p.hbm_in("shw1", [2 * D, 64]); p.hbm_in("shb1", [64])
    p.hbm_in("shw2", [64, 3072]); p.hbm_in("shb2", [3072])
    p.hbm_in("swv", [1])
    p.hbm_in("rfw1", [D, 2 * D]); p.hbm_in("rfb1", [2 * D])
    p.hbm_in("rfw2", [2 * D, D]); p.hbm_in("rfb2", [D])
    p.hbm_in("sel8", [8, 4 * 128])
    p.hbm_in("ident", [128, 128])


def body_l3_ffn(p, cb, xn, fv, ps):
    DT = 0.1
    xfd = p.hbm_tmp("xfd", [128, NCH * T], BF16)
    integ = xn
    velb = p.sbuf("velb", [128, NCH * T], BF16); vel = V3(velb, NCH, T)
    h1rb = p.sbuf("h1rb", [128, 2 * T], BF16); h1r = V3(h1rb, 2, T)
    wbig = p.sbuf("wbig3", [128, 16, 512], BF16)
    heb = V3(fv.t, 4, T)
    h1b = V3(fv.t, 4, T, base=4 * T)
    ltm = p.sbuf("ltm", [128, 16 * 8], F32)
    lt2 = p.sbuf("lt2", [128, 16 * 8], F32)
    oh1 = p.sbuf("oh1", [128, 16 * 8], BF16)
    oh2 = p.sbuf("oh2", [128, 16 * 8], BF16)
    m1c = p.sbuf("m1c", [128, 16], F32)
    m2c = p.sbuf("m2c", [128, 16], F32)
    rb2b = p.sbuf("rb2b", [128, 8], F32)
    o1fm = p.sbuf("o1fm", [8, T], BF16)
    o2fm = p.sbuf("o2fm", [8, T], BF16)
    wa128 = p.sbuf("wa128", [128, T], BF16)
    halt128 = p.sbuf("halt128", [128, T], BF16)
    waR = p.sbuf("waR", [1, T], F32)
    selb = p.sbuf("selb8", [8, 4 * 128], BF16)
    idb = p.sbuf("idb3", [128, 128], BF16)
    swc = p.sbuf("swc", [128, 2], F32)
    tA = [p.sbuf(f"tA{i}", [128, 512], BF16) for i in range(6)]
    tS = p.sbuf("tS", [128, 512], F32)
    tS2 = p.sbuf("tS2", [128, 512], F32)
    p.sbuf("wsh3", [128, 1, 384], BF16)
    p.sbuf("b2e3", [128, 1, 384], BF16)
    psT = p.psum("psT3", [16, 128], BF16)
    p.dma("gpsimd", selb[:, :], p.t["sel8"])
    p.dma("gpsimd", idb[:, :], p.t["ident"])
    p.dma("sync", xfd, xn[:, 0:NCH, 0:T])

    iwc = load_col(p, "iwc", p.t["iw"], D)
    muc = load_col(p, "muc", p.t["mu"], D)
    rb1c = load_col(p, "rb1c", p.t["rb1"], 256)
    hb1c = load_col(p, "hb1c", p.t["hb1"], 256)
    hb2c = load_col(p, "hb2c", p.t["hb2"], 1)
    b1ac = load_col(p, "b1ac", p.t["b1all"], 512)
    shb1c = load_col(p, "shb1c", p.t["shb1"], 64)
    shb2c = load_col(p, "shb2c", p.t["shb2"], 3072)
    rfb1c = load_col(p, "rfb1c", p.t["rfb1"], 2 * D)
    rfb2c = load_col(p, "rfb2c", p.t["rfb2"], D)

    p.dma("sync", swc[0:1, 0:1], p.t["swv"])
    p.op("scalar", lambda e: e.activation(swc[0:1, 0:1], swc[0:1, 0:1], AF.Sigmoid))
    p.op("vector", lambda e: e.tensor_copy(cb["rowbf"][0:1, 0:1], swc[0:1, 0:1]))
    p.op("tensor", lambda e: e.matmul(ps[0:128, 0:1], cb["ones1"][0:1, 0:128],
                                      cb["rowbf"][0:1, 0:1], start=True, stop=True))
    p.op("scalar", lambda e: e.copy(swc[:, 0:1], ps[0:128, 0:1]))
    p.op("vector", lambda e: e.tensor_scalar(swc[:, 1:2], swc[:, 0:1], -1.0, 1.0,
                                             op0=ALU.mult, op1=ALU.add))

    linear(p, "r1", p.t["rw1"], D, 256, integ, h1r, T, wbig, ps[:, 0:512],
           bias=rb1c, act=AF.Gelu)
    p.dma("sync", rb2b[0:1, 0:8], p.t["rb2"])
    p.op("vector", lambda e: e.tensor_copy(cb["rowbf"][0:1, 0:8], rb2b[0:1, 0:8]))
    p.op("tensor", lambda e: e.matmul(ps[0:128, 0:8], cb["ones1"][0:1, 0:128],
                                      cb["rowbf"][0:1, 0:8], start=True, stop=True))
    p.op("scalar", lambda e: e.copy(rb2b[:, 0:8], ps[0:128, 0:8]))
    load_w(p, None, p.t["rw2"], 256, 8, sb=wbig)
    for tt in range(16):
        for kc in range(2):
            p.op("tensor", lambda e, tt=tt, kc=kc: e.matmul(
                ps[0:128, 0:8], h1r[:, kc, tt * 128:tt * 128 + 128],
                wbig[:, kc, 0:8], start=(kc == 0), stop=(kc == 1)))
        p.op("vector", lambda e, tt=tt: e.tensor_add(
            ltm[:, tt * 8:tt * 8 + 8], ps[0:128, 0:8], rb2b[:, 0:8]))
    lv = V3(ltm, 16, 8)
    l2v = V3(lt2, 16, 8)
    p.op("vector", lambda e: e.tensor_reduce(
        m1c[:, 0:16], lv[:, 0:16, 0:8], axis=AX.X, op=ALU.max))
    m1b = bass.AP(m1c, 0, [[16, 128], [1, 16], [0, 8]])
    m2b = bass.AP(m2c, 0, [[16, 128], [1, 16], [0, 8]])
    p.op("vector", lambda e: e.tensor_tensor(
        V3(oh1, 16, 8)[:, 0:16, 0:8], lv[:, 0:16, 0:8], m1b, op=ALU.is_ge))
    p.op("vector", lambda e: e.scalar_tensor_tensor(
        l2v[:, 0:16, 0:8], V3(oh1, 16, 8)[:, 0:16, 0:8], -1e30, lv[:, 0:16, 0:8],
        op0=ALU.mult, op1=ALU.add))
    p.op("vector", lambda e: e.tensor_reduce(
        m2c[:, 0:16], l2v[:, 0:16, 0:8], axis=AX.X, op=ALU.max))
    p.op("vector", lambda e: e.tensor_tensor(
        V3(oh2, 16, 8)[:, 0:16, 0:8], l2v[:, 0:16, 0:8], m2b, op=ALU.is_ge))
    p.op("vector", lambda e: e.tensor_sub(m1c[:, 0:16], m1c[:, 0:16], m2c[:, 0:16]))
    p.op("scalar", lambda e: e.activation(m1c[:, 0:16], m1c[:, 0:16], AF.Sigmoid))
    for tt in range(16):
        p.op("tensor", lambda e, tt=tt: e.transpose(
            psT[0:8, 0:128], oh1[:, tt * 8:tt * 8 + 8], idb[:, :]))
        p.op("vector", lambda e, tt=tt: e.tensor_copy(
            o1fm[0:8, tt * 128:tt * 128 + 128], psT[0:8, 0:128]))
        p.op("tensor", lambda e, tt=tt: e.transpose(
            psT[0:8, 0:128], oh2[:, tt * 8:tt * 8 + 8], idb[:, :]))
        p.op("vector", lambda e, tt=tt: e.tensor_copy(
            o2fm[0:8, tt * 128:tt * 128 + 128], psT[0:8, 0:128]))
    p.op("vector", lambda e: e.tensor_copy(oh1[:, 0:16], m1c[:, 0:16]))
    p.op("tensor", lambda e: e.transpose(psT[0:16, 0:128], oh1[:, 0:16], idb[:, :]))
    p.op("vector", lambda e: e.tensor_copy(oh2[0:16, 0:128], psT[0:16, 0:128]))
    p.dma("gpsimd", waR[0:1, 0:T], oh2[0:16, 0:128])
    bcast_row(p, cb, waR[0:1, 0:T], wa128, T)

    for it in range(2):
        linear(p, "h1", p.t["hw1"], D, 256, integ, h1r, T, wbig, ps[:, 0:512],
               bias=hb1c, act=AF.Gelu)
        load_w(p, None, p.t["hw2"], 256, 1, sb=wbig)
        for tg in range(4):
            for kc in range(2):
                p.op("tensor", lambda e, tg=tg, kc=kc: e.matmul(
                    ps[0:1, 1024:1536], wbig[:, kc, 0:1],
                    h1r[:, kc, tg * 512:tg * 512 + 512],
                    start=(kc == 0), stop=(kc == 1)))
            p.op("scalar", lambda e, tg=tg: e.activation(
                cb["strow"][0:1, tg * 512:tg * 512 + 512], ps[0:1, 1024:1536],
                AF.Sigmoid, bias=hb2c[0:1, 0:1], scale=1.0))
        bcast_row(p, cb, cb["strow"][0:1, 0:T], halt128, T)

        def ctxrhs(kc, t0, tw):
            if kc < NCH:
                return integ[:, kc, t0:t0 + tw]
            return vel[:, kc - NCH, t0:t0 + tw]

        if it == 0:
            p.op("vector", lambda e: e.memset(velb[:, :], 0.0))
        linear(p, "he", p.t["w1all"], 2 * D, 512, None, heb, T, wbig,
               ps[:, 0:512], bias=b1ac, act=AF.Gelu, rhs_fn=ctxrhs)
        linear(p, "hs", p.t["shw1"], 2 * D, 64, None, h1r, T, wbig,
               ps[:, 0:512], bias=shb1c, act=AF.Gelu, rhs_fn=ctxrhs)

        for k, ofm, dstb in ((0, o1fm, h1b), (1, o2fm, heb)):
            for c in range(4):
                for tg in range(4):
                    p.op("tensor", lambda e, c=c, tg=tg, ofm=ofm: e.matmul(
                        ps[0:128, 1024:1536], selb[0:8, c * 128:c * 128 + 128],
                        ofm[0:8, tg * 512:tg * 512 + 512], start=True, stop=True))
                    p.op("vector", lambda e, c=c, tg=tg, dstb=dstb: e.tensor_mul(
                        dstb[:, c, tg * 512:tg * 512 + 512],
                        heb[:, c, tg * 512:tg * 512 + 512], ps[0:128, 1024:1536]))

        for slab in range(NCH):
            load_w(p, None, p.t["w2int"], 512, 3072, m0=slab * 384,
                   m1=slab * 384 + 384, sb=wbig)
            load_w(p, None, p.t["shw2"], 64, 3072, m0=slab * 384,
                   m1=slab * 384 + 384, sb=p.t["wsh3"])
            b2sb = p.t["b2e3"]
            load_w(p, None, p.t["b2eint"], 8, 3072, m0=slab * 384,
                   m1=slab * 384 + 384, sb=b2sb)
            for tg in range(4):
                t0 = tg * 512
                res = []
                for src_i in range(3):
                    for part in range(3):
                        psl = ps[0:128, part * 512:part * 512 + 512]
                        if src_i < 2:
                            hb_ = h1b if src_i == 0 else heb
                            for kc in range(4):
                                p.op("tensor", lambda e, kc=kc, part=part, psl=psl,
                                     t0=t0, hb_=hb_: e.matmul(
                                         psl, wbig[:, kc, part * 128:part * 128 + 128],
                                         hb_[:, kc, t0:t0 + 512],
                                         start=(kc == 0), stop=False))
                            ofm = o1fm if src_i == 0 else o2fm
                            p.op("tensor", lambda e, part=part, psl=psl, t0=t0,
                                 ofm=ofm, b2sb=b2sb: e.matmul(
                                     psl, b2sb[0:8, 0, part * 128:part * 128 + 128],
                                     ofm[0:8, t0:t0 + 512], start=False, stop=True))
                        else:
                            p.op("tensor", lambda e, part=part, psl=psl, t0=t0: e.matmul(
                                psl, p.t["wsh3"][0:64, 0, part * 128:part * 128 + 128],
                                h1r[0:64, 0, t0:t0 + 512], start=True, stop=True))
                    sb2col = shb2c if src_i == 2 else None
                    xa, va = tA[2 * src_i], tA[2 * src_i + 1]

                    def bias3(part, slab=slab):
                        if sb2col is None:
                            return 0.0
                        return sb2col[0:128, (slab * 384 + part * 128) // 128:
                                      (slab * 384 + part * 128) // 128 + 1]
                    p.op("scalar", lambda e, b_=bias3(0): e.activation(
                        tS[:, 0:512], ps[0:128, 0:512], AF.Sigmoid, bias=b_, scale=1.0))
                    p.op("scalar", lambda e, b_=bias3(1): e.activation(
                        tS2[:, 0:512], ps[0:128, 512:1024], AF.Identity,
                        bias=b_, scale=1.0))
                    p.op("scalar", lambda e: e.activation(
                        va[:, 0:512], tS2[:, 0:512], AF.Abs))
                    p.op("scalar", lambda e: e.activation(
                        va[:, 0:512], va[:, 0:512], AF.Exp, bias=0.0, scale=-1.0))
                    p.op("scalar", lambda e: e.activation(
                        va[:, 0:512], va[:, 0:512], AF.Ln, bias=1.0, scale=1.0))
                    p.op("vector", lambda e: e.tensor_single_scalar(
                        tS2[:, 0:512], tS2[:, 0:512], 0.0, op=ALU.max))
                    p.op("vector", lambda e: e.tensor_add(
                        tS2[:, 0:512], tS2[:, 0:512], va[:, 0:512]))
                    p.op("vector", lambda e, slab=slab, t0=t0: e.tensor_scalar(
                        va[:, 0:512], integ[:, slab, t0:t0 + 512],
                        muc[:, slab:slab + 1], None, op0=ALU.subtract))
                    p.op("vector", lambda e: e.tensor_mul(
                        tS2[:, 0:512], tS2[:, 0:512], va[:, 0:512]))
                    p.op("vector", lambda e, slab=slab, t0=t0: e.tensor_mul(
                        tS[:, 0:512], tS[:, 0:512], vel[:, slab, t0:t0 + 512]))
                    p.op("vector", lambda e, va=va: e.tensor_sub(
                        va[:, 0:512], tS[:, 0:512], tS2[:, 0:512]))
                    p.op("scalar", lambda e, b_=bias3(2): e.activation(
                        tS[:, 0:512], ps[0:128, 1024:1536], AF.Sigmoid,
                        bias=b_, scale=1.0))
                    p.op("vector", lambda e, va=va: e.tensor_mul(
                        tS[:, 0:512], tS[:, 0:512], va[:, 0:512]))
                    p.op("vector", lambda e, xa=xa, slab=slab, t0=t0: e.scalar_tensor_tensor(
                        xa[:, 0:512], tS[:, 0:512], DT, integ[:, slab, t0:t0 + 512],
                        op0=ALU.mult, op1=ALU.add))
                for xa1, xa2, xsh, dstv in ((tA[0], tA[2], tA[4], 0),
                                            (tA[1], tA[3], tA[5], 1)):
                    p.op("vector", lambda e, xa1=xa1, xa2=xa2: e.tensor_sub(
                        tS[:, 0:512], xa1[:, 0:512], xa2[:, 0:512]))
                    p.op("vector", lambda e, t0=t0: e.tensor_mul(
                        tS[:, 0:512], tS[:, 0:512], wa128[:, t0:t0 + 512]))
                    p.op("vector", lambda e, xa2=xa2: e.tensor_add(
                        tS[:, 0:512], tS[:, 0:512], xa2[:, 0:512]))
                    p.op("vector", lambda e: e.tensor_scalar(
                        tS[:, 0:512], tS[:, 0:512], swc[:, 1:2], None, op0=ALU.mult))
                    p.op("vector", lambda e, xsh=xsh: e.tensor_scalar(
                        tS2[:, 0:512], xsh[:, 0:512], swc[:, 0:1], None, op0=ALU.mult))
                    p.op("vector", lambda e: e.tensor_add(
                        tS[:, 0:512], tS[:, 0:512], tS2[:, 0:512]))
                    tgt = integ if dstv == 0 else vel
                    p.op("vector", lambda e, tgt=tgt, slab=slab, t0=t0: e.tensor_sub(
                        tS[:, 0:512], tS[:, 0:512], tgt[:, slab, t0:t0 + 512]))
                    p.op("vector", lambda e, t0=t0: e.tensor_mul(
                        tS[:, 0:512], tS[:, 0:512], halt128[:, t0:t0 + 512]))
                    p.op("vector", lambda e, tgt=tgt, slab=slab, t0=t0: e.tensor_add(
                        tgt[:, slab, t0:t0 + 512], tgt[:, slab, t0:t0 + 512],
                        tS[:, 0:512]))

    hr = V3(fv.t, 16, 1024)
    xstmp = h1r
    for half in range(2):
        h0 = half * 1024
        linear(p, "rf1", p.t["rfw1"], D, 2 * D, integ, hr, 1024, wbig,
               ps[:, 0:512], bias=rfb1c, act=AF.Gelu, tok0=h0, dst_tok0=0)
        for mb0 in range(0, D, 512):
            load_w(p, None, p.t["rfw2"], 2 * D, D, m0=mb0, m1=mb0 + 512, sb=wbig)
            for mc0 in range(mb0, mb0 + 512, 128):
                mc = mc0 // 128
                for tb in range(2):
                    t0 = tb * 512
                    psl = ps[0:128, 0:512]
                    for kc in range(16):
                        p.op("tensor", lambda e, kc=kc, mc0=mc0, mb0=mb0, t0=t0,
                             psl=psl: e.matmul(
                                 psl, wbig[:, kc, mc0 - mb0:mc0 - mb0 + 128],
                                 hr[:, kc, t0:t0 + 512],
                                 start=(kc == 0), stop=(kc == 15)))
                    p.dma("sync", xstmp[:, 0, 0:512] if False else h1rb[0:128, 0:512],
                          bass.AP(xfd.tensor, xfd.offset + mc * T + h0 + t0,
                                  [[NCH * T, 128], [1, 512]]))
                    p.op("vector", lambda e, mc=mc: e.tensor_scalar(
                        tS[:, 0:512], psl, rfb2c[:, mc:mc + 1], iwc[:, mc:mc + 1],
                        op0=ALU.add, op1=ALU.mult))
                    p.op("vector", lambda e, mc=mc, t0=t0, h0=h0: e.tensor_add(
                        vel[:, mc, h0 + t0:h0 + t0 + 512], tS[:, 0:512],
                        h1rb[0:128, 0:512]))
    return vel


def decl_l0_attn(p):
    for w in ("wkT", "wvT", "wrT", "woT"):
        p.hbm_in(w, [D, D])
    p.hbm_in("tdec", [16, 64])
    p.hbm_in("tfir", [16, 64])
    p.hbm_in("m64", [128, 64])
    p.hbm_in("ones64", [64, 64])
    p.hbm_in("ident", [128, 128])


def body_l0_attn(p, cb, xn, fv_none, ps):
    C = 64
    NCK = T // C
    wbuf = p.sbuf("wbuf", [128, NCH, 128], BF16)
    kfb = p.sbuf("kfb", [128, NCH * T], BF16); kf = V3(kfb, NCH, T)
    rfb = p.sbuf("rfb", [128, NCH * T], BF16); rf = V3(rfb, NCH, T)
    vtm = p.sbuf("vtm", [128, 16 * 1024], BF16)
    dpow = p.sbuf("dpow", [128, NCH * 81], F32); dpv = V3(dpow, NCH, 81)
    dfm = p.sbuf("dfm", [128, NCH], F32)
    dinv = p.sbuf("dinv", [128, NCH], F32)
    tffm = p.sbuf("tffm", [128, NCH], F32)
    pdk = p.sbuf("pdk", [128, NCH * 64], BF16); pdkv = V3(pdk, NCH, 64)
    pdr = p.sbuf("pdr", [128, NCH * 64], BF16); pdrv = V3(pdr, NCH, 64)
    po3 = p.sbuf("po3", [128, NCH * 48], BF16); po3v = V3(po3, NCH, 48)
    pro = p.sbuf("pro", [128, NCH * 48], BF16); prov = V3(pro, NCH, 48)
    pows = p.sbuf("pows", [128, 1024], BF16)
    rowbig = p.sbuf("rowbig", [1, 3072], F32)
    drow = rowbig[0:1, 0:1024]
    tfrow = rowbig[0:1, 1024:2048]
    powrow = rowbig[0:1, 2048:3072]
    d64b = p.sbuf("d64b", [64, 1024], BF16)
    d32c = p.sbuf("d32c", [64, 1], F32)
    ones64 = p.sbuf("ones64s", [64, 64], BF16)
    sdup = p.sbuf("sdup", [64, 1024], F32)
    sbf = p.sbuf("sbf", [128, 1024], BF16)
    ksb = p.sbuf("ksb", [128, 1024], BF16)
    rd = p.sbuf("rd", [128, NCH * 64], BF16); rdv = V3(rd, NCH, 64)
    rc = p.sbuf("rc", [128, NCH * 64], BF16); rcv = V3(rc, NCH, 64)
    ko = p.sbuf("ko", [128, NCH * 96], BF16); kov = V3(ko, NCH, 96)
    ro = p.sbuf("ro", [128, NCH * 48], BF16); rov = V3(ro, NCH, 48)
    adg = p.sbuf("adg", [128, 1024], BF16)
    aof = p.sbuf("aof", [128, 768], BF16)
    kdv = V3(aof, NCH, 64)
    m64 = p.sbuf("m64s", [128, 64], BF16)
    idb = p.sbuf("idb0", [128, 128], BF16)
    p.dma("gpsimd", m64[:, :], p.t["m64"])
    p.dma("gpsimd", idb[:, :], p.t["ident"])
    p.dma("gpsimd", ones64[:, :], p.t["ones64"])
    p.dma("sync", drow, p.t["tdec"])
    p.dma("sync", tfrow, p.t["tfir"])
    for c in range(NCH):
        p.dma("sync", dfm[0:128, c:c + 1],
              bass.AP(p.t["tdec"].tensor, p.t["tdec"].offset + c * 128, [[1, 128], [1, 1]]))
        p.dma("sync", tffm[0:128, c:c + 1],
              bass.AP(p.t["tfir"].tensor, p.t["tfir"].offset + c * 128, [[1, 128], [1, 1]]))
    psbf = p.psum("psbf0", [128, 1024], BF16)
    sbd = p.hbm_tmp("sbd", [64, 1024], BF16)

    p.op("vector", lambda e: e.reciprocal(dinv[:, :], dfm[:, :]))
    p.op("vector", lambda e: e.memset(dpv[:, 0:NCH, 16:17], 1.0))
    for n in range(17, 81):
        p.op("vector", lambda e, n=n: e.tensor_mul(
            dpv[:, 0:NCH, n:n + 1], dpv[:, 0:NCH, n - 1:n], dfm[:, 0:NCH]))
    for n in range(15, -1, -1):
        p.op("vector", lambda e, n=n: e.tensor_mul(
            dpv[:, 0:NCH, n:n + 1], dpv[:, 0:NCH, n + 1:n + 2], dinv[:, 0:NCH]))
    for j in range(64):
        p.op("vector", lambda e, j=j: e.tensor_mul(
            pdkv[:, 0:NCH, j:j + 1], dpv[:, 0:NCH, 24 - j % 16:25 - j % 16],
            tffm[:, 0:NCH]))
        p.op("vector", lambda e, j=j: e.tensor_copy(
            pdrv[:, 0:NCH, j:j + 1], dpv[:, 0:NCH, j % 16 + 8:j % 16 + 9]))
    for j in range(48):
        p.op("vector", lambda e, j=j: e.tensor_mul(
            po3v[:, 0:NCH, j:j + 1], dpv[:, 0:NCH, 64 - j:65 - j], tffm[:, 0:NCH]))
        p.op("vector", lambda e, j=j: e.tensor_copy(
            prov[:, 0:NCH, j:j + 1], dpv[:, 0:NCH, 16 + j % 16:17 + j % 16]))
    p.op("vector", lambda e: e.tensor_copy(powrow, tfrow))
    for r in range(63, -1, -1):
        for rr in (r, r + 64):
            p.dma("gpsimd", pows[rr:rr + 1, 0:1024], powrow)
        if r > 0:
            p.op("vector", lambda e: e.tensor_mul(powrow, powrow, drow))
    p.op("vector", lambda e: e.tensor_copy(powrow, drow))
    for _ in range(6):
        p.op("vector", lambda e: e.tensor_mul(powrow, powrow, powrow))
    for h in range(16):
        p.dma("sync", d32c[0:64, 0:1], rowbig[0:1, 2048 + h * 64:2048 + h * 64 + 64])
        p.op("scalar", lambda e, h=h: e.activation(
            d64b[0:64, h * 64:h * 64 + 64], ones64[:, :], AF.Identity,
            bias=0.0, scale=d32c[0:64, 0:1]))

    linear(p, "k", p.t["wkT"], D, D, xn, kf, T, wbuf, ps[:, 0:512], mblk=128)
    linear(p, "r", p.t["wrT"], D, D, xn, rf, T, wbuf, ps[:, 0:512], act=AF.Sigmoid,
           mblk=128)
    for mb0 in range(0, D, 128):
        load_w(p, None, p.t["wvT"], D, D, m0=mb0, m1=mb0 + 128, sb=wbuf)
        for tau in range(32):
            psl = ps[0:64, 0:128] if tau % 2 == 0 else ps[64:128, 0:128]
            for kc in range(NCH):
                p.op("tensor", lambda e, kc=kc, tau=tau, psl=psl: e.matmul(
                    psl, xn[:, kc, tau * 64:tau * 64 + 64],
                    wbuf[:, kc, 0:128],
                    start=(kc == 0), stop=(kc == NCH - 1)))
            pb = 64 * (tau % 2)
            p.op("scalar", lambda e, tau=tau, mb0=mb0, psl=psl, pb=pb: e.copy(
                bass.AP(vtm, pb * 16 * 1024 + (tau // 2) * 1024 + mb0,
                        [[16 * 1024, 64], [1, 128]]),
                psl))

    p.op("vector", lambda e: e.memset(sdup[:, :], 0.0))
    p.op("vector", lambda e: e.memset(sbf[:, :], 0.0))

    yv = kf
    PSD, PSF, PSY, PSS = 0, 1024, 1792, 0
    for ck in range(NCK):
        T0 = ck * C
        pb = 64 * (ck % 2)
        tile = ck // 2
        vsl = lambda j0, jn, h: bass.AP(
            vtm, pb * 16 * 1024 + tile * 1024 + h * 64 + j0 * 16 * 1024,
            [[16 * 1024, jn], [1, 64]])
        for h in range(16):
            hb = 64 * (h % 2)
            p.op("tensor", lambda e, h=h, hb=hb, T0=T0, pb=pb: e.transpose(
                psbf[pb:pb + 64, h * 64:h * 64 + 64],
                bass.AP(kfb, hb * NCH * T + (h // 2) * T + T0, [[NCH * T, 64], [1, 64]]),
                idb[hb:hb + 64, hb:hb + 64]))
        p.op("vector", lambda e, pb=pb: e.tensor_copy(ksb[pb:pb + 64, 0:1024],
                                                      psbf[pb:pb + 64, 0:1024]))
        p.op("vector", lambda e, pb=pb: e.tensor_mul(
            ksb[pb:pb + 64, 0:1024], ksb[pb:pb + 64, 0:1024], pows[pb:pb + 64, 0:1024]))
        p.op("vector", lambda e, T0=T0: e.tensor_mul(
            kdv[:, 0:NCH, 0:64], kf[:, 0:NCH, T0:T0 + 64], pdkv[:, 0:NCH, 0:64]))
        p.op("vector", lambda e, T0=T0: e.tensor_mul(
            rdv[:, 0:NCH, 0:64], rf[:, 0:NCH, T0:T0 + 64], pdrv[:, 0:NCH, 0:64]))
        p.op("vector", lambda e, T0=T0: e.tensor_mul(
            rcv[:, 0:NCH, 0:64], rf[:, 0:NCH, T0:T0 + 64], dpv[:, 0:NCH, 17:81]))
        for I in range(1, 4):
            p.op("vector", lambda e, T0=T0, I=I: e.tensor_mul(
                kov[:, 0:NCH, 16 * (I - 1) * 2 - 16 * (I - 1) * 0:0] if False else
                kov[:, 0:NCH, (I - 1) * (I) * 8:(I - 1) * I * 8 + 16 * I],
                kf[:, 0:NCH, T0:T0 + 16 * I],
                po3v[:, 0:NCH, 48 - 16 * I:48]))
        p.op("vector", lambda e, T0=T0: e.tensor_mul(
            rov[:, 0:NCH, 0:48], rf[:, 0:NCH, T0 + 16:T0 + 64], prov[:, 0:NCH, 0:48]))
        for h in range(16):
            hb = 64 * (h % 2)
            hc = h // 2
            p.op("tensor", lambda e, h=h, hb=hb, hc=hc, pb=pb: e.matmul(
                ps[pb:pb + 64, PSD + h * 64:PSD + h * 64 + 64],
                kdv[hb:hb + 64, hc, 0:64], rdv[hb:hb + 64, hc, 0:64],
                start=True, stop=True))
            for I in range(1, 4):
                o0 = (I - 1) * I * 8
                p.op("tensor", lambda e, h=h, hb=hb, hc=hc, pb=pb, I=I, o0=o0: e.matmul(
                    ps[pb:pb + 16 * I, PSF + h * 48 + 16 * (I - 1):
                       PSF + h * 48 + 16 * I],
                    kov[hb:hb + 64, hc, o0:o0 + 16 * I],
                    rov[hb:hb + 64, hc, 16 * (I - 1):16 * I],
                    start=True, stop=True))
        p.op("scalar", lambda e, pb=pb: e.copy(adg[pb:pb + 64, 0:1024],
                                               ps[pb:pb + 64, PSD:PSD + 1024]))
        p.op("vector", lambda e, pb=pb: e.tensor_mul(
            adg[pb:pb + 64, 0:1024], adg[pb:pb + 64, 0:1024],
            bass.AP(m64, pb * 64, [[64, 64], [0, 16], [1, 64]])))
        p.op("scalar", lambda e, pb=pb: e.copy(aof[pb:pb + 48, 0:768],
                                               ps[pb:pb + 48, PSF:PSF + 768]))
        for h in range(16):
            hb = 64 * (h % 2)
            hc = h // 2
            p.op("tensor", lambda e, h=h, hb=hb, hc=hc: e.matmul(
                ps[0:64, PSY + h * 64:PSY + h * 64 + 64],
                sbf[hb:hb + 64, h * 64:h * 64 + 64],
                rcv[hb:hb + 64, hc, 0:64], start=True, stop=False))
            for I in range(1, 4):
                p.op("tensor", lambda e, h=h, pb=pb, tile=tile, I=I: e.matmul(
                    ps[0:64, PSY + h * 64 + 16 * I:PSY + h * 64 + 16 * I + 16],
                    bass.AP(vtm, pb * 16 * 1024 + tile * 1024 + h * 64,
                            [[16 * 1024, 16 * I], [1, 64]]),
                    aof[pb:pb + 16 * I, h * 48 + 16 * (I - 1):h * 48 + 16 * I],
                    start=False, stop=False))
            p.op("tensor", lambda e, h=h, pb=pb, tile=tile: e.matmul(
                ps[0:64, PSY + h * 64:PSY + h * 64 + 64],
                bass.AP(vtm, pb * 16 * 1024 + tile * 1024 + h * 64,
                        [[16 * 1024, 64], [1, 64]]),
                adg[pb:pb + 64, h * 64:h * 64 + 64], start=False, stop=True))
            p.op("tensor", lambda e, h=h, pb=pb, tile=tile: e.matmul(
                ps[0:64, PSS + h * 64:PSS + h * 64 + 64],
                ksb[pb:pb + 64, h * 64:h * 64 + 64],
                bass.AP(vtm, pb * 16 * 1024 + tile * 1024 + h * 64,
                        [[16 * 1024, 64], [1, 64]]),
                start=True, stop=True))
        p.op("scalar", lambda e, T0=T0: e.copy(
            bass.AP(kfb, T0, [[NCH * T, 64], [T, NCH], [1, 64]]),
            bass.AP(ps, PSY, [[3584, 64], [128, NCH], [1, 64]])))
        p.op("scalar", lambda e, T0=T0: e.copy(
            bass.AP(kfb, 64 * NCH * T + T0, [[NCH * T, 64], [T, NCH], [1, 64]]),
            bass.AP(ps, PSY + 64, [[3584, 64], [128, NCH], [1, 64]])))
        p.op("vector", lambda e: e.tensor_mul(sdup[0:64, 0:1024], sdup[0:64, 0:1024],
                                              d64b[0:64, 0:1024]))
        p.op("vector", lambda e: e.tensor_add(sdup[0:64, 0:1024], sdup[0:64, 0:1024],
                                              ps[0:64, PSS:PSS + 1024]))
        p.op("vector", lambda e: e.tensor_copy(sbf[0:64, 0:1024], sdup[0:64, 0:1024]))
        p.dma("sync", sbd, sbf[0:64, 0:1024])
        p.dma("sync", sbf[64:128, 0:1024], sbd)
    if DBG:
        p.dump("dbg_y", yv[:, 0:NCH, 0:T], [128, NCH * T], BF16)
    fo = V3(rfb, NCH, T)
    linear(p, "o", p.t["woT"], D, D, yv, fo, T, wbuf, ps[:, 0:512], mblk=128)
    return fo


def decl_l2_attn(p):
    p.hbm_in("wq2", [D, D])
    p.hbm_in("wk2", [D, D])
    p.hbm_in("wv2", [D, D])
    p.hbm_in("wproj", [D, D])
    p.hbm_in("ekrT", [128, 2176])
    p.hbm_in("evr", [2048, 64])
    p.hbm_in("trilm", [128, 128])
    p.hbm_in("ident", [128, 128])


def body_l2_attn(p, cb, xn, fv_none, ps):
    wbuf = p.sbuf("wbuf", [128, NCH, 512], BF16)
    qb = p.sbuf("qb2", [128, NCH * T], BF16); qv = V3(qb, NCH, T)
    kb = p.sbuf("kb2", [128, NCH * T], BF16); kv = V3(kb, NCH, T)
    vtm = p.sbuf("vtm2", [128, 16 * 1024], BF16); vtv = V3(vtm, 16, 1024)
    ekr = p.sbuf("ekr", [128, 2176], BF16)
    evr = p.sbuf("evr2", [128, 16 * 64], BF16); evv = V3(evr, 16, 64)
    attn = p.sbuf("attn2", [128, 2048], BF16)
    a2 = p.sbuf("a22", [128, 2048], BF16)
    gsb = p.sbuf("gsb", [128, 2048], BF16)
    attT = p.sbuf("attT2", [128, 512], BF16)
    prb = p.sbuf("prb", [128, 512], BF16)
    zrs = p.sbuf("zrs", [128, 512], BF16)
    denc = p.sbuf("denc2", [128, 1], F32)
    trilm = p.sbuf("trilm2", [128, 128], F32)
    idb = p.sbuf("idb2", [128, 128], BF16)
    expb = p.sbuf("expb2", [128, 2048], F32)
    psbf = p.psum("psbf2", [128, 512], BF16)
    prd = p.hbm_tmp("prd", [128, 2176], BF16)
    grd = p.hbm_tmp("grd", [128, 2176], BF16)
    p.dma("gpsimd", ekr[:, :], p.t["ekrT"])
    p.dma("gpsimd", evr[:, 0:16 * 64], p.t["evr"].rearrange("(c p) d -> p c d", p=128))
    p.dma("sync", trilm[:, :], p.t["trilm"])
    p.dma("gpsimd", idb[:, :], p.t["ident"])
    p.op("vector", lambda e: e.memset(zrs[:, :], 0.0))

    linear(p, "q", p.t["wq2"], D, D, xn, qv, T, wbuf, ps[:, 0:512])
    linear(p, "k", p.t["wk2"], D, D, xn, kv, T, wbuf, ps[:, 0:512],
           act=AF.Identity, act_scale=0.125)
    for mb0 in range(0, D, 512):
        load_w(p, None, p.t["wv2"], D, D, m0=mb0, m1=mb0 + 512, sb=wbuf)
        for tt in range(16):
            psl = ps[0:128, 0:512]
            for kc in range(NCH):
                p.op("tensor", lambda e, kc=kc, tt=tt, psl=psl: e.matmul(
                    psl, xn[:, kc, tt * 128:tt * 128 + 128],
                    wbuf[:, kc, 0:512],
                    start=(kc == 0), stop=(kc == NCH - 1)))
            p.op("scalar", lambda e, tt=tt, mb0=mb0, psl=psl: e.copy(
                vtv[:, tt, mb0:mb0 + 512], psl))

    for qt in range(16):
        t0 = qt * 128
        L = (qt + 1) * 128
        c0p = 1920 - t0
        for h in range(16):
            hb = 64 * (h % 2)
            hc = h // 2
            ql = bass.AP(qb, hb * NCH * T + hc * T + t0, [[NCH * T, 64], [1, 128]])
            for cc0 in range(c0p - c0p % 512, 2176, 512):
                cw = min(512, 2176 - cc0)
                p.op("tensor", lambda e, ql=ql, cc0=cc0, cw=cw, hb=hb: e.matmul(
                    ps[0:128, 2560:2560 + cw], ql,
                    bass.AP(ekr, hb * 2176 + cc0, [[2176, 64], [1, cw]]),
                    start=True, stop=True))
                p.op("scalar", lambda e, cw=cw: e.copy(prb[:, 0:cw],
                                                       ps[0:128, 2560:2560 + cw]))
                p.dma("sync", bass.AP(prd.tensor, prd.offset + cc0, [[2176, 128], [1, cw]]),
                      prb[:, 0:cw])
            for g0 in range(0, L, 512):
                gw = min(512, L - g0)
                p.op("tensor", lambda e, ql=ql, g0=g0, gw=gw, hb=hb, hc=hc: e.matmul(
                    ps[0:128, g0:g0 + gw], ql,
                    bass.AP(kb, hb * NCH * T + hc * T + g0, [[NCH * T, 64], [1, gw]]),
                    start=True, stop=True))
            p.dma("sync", a2[:, 0:L],
                  bass.AP(prd.tensor, prd.offset + 2047 - t0, [[2175, 128], [1, L]]))
            p.op("vector", lambda e, L=L: e.tensor_add(
                ps[0:128, 0:L], ps[0:128, 0:L], a2[:, 0:L]))
            p.op("scalar", lambda e, L=L: e.activation(
                expb[:, 0:L], ps[0:128, 0:L], AF.Exp, bias=0.0, scale=1.0))
            p.op("vector", lambda e, L=L: e.tensor_mul(
                expb[:, L - 128:L], expb[:, L - 128:L], trilm[:, :]))
            p.op("vector", lambda e, L=L: e.tensor_reduce(
                denc[:, 0:1], expb[:, 0:L], axis=AX.X, op=ALU.add))
            p.op("vector", lambda e: e.reciprocal(denc[:, 0:1], denc[:, 0:1]))
            p.op("scalar", lambda e, L=L: e.activation(
                attn[:, 0:L], expb[:, 0:L], AF.Identity, bias=0.0,
                scale=denc[:, 0:1]))
            for z0 in range(0, 2047 - t0, 512):
                zw = min(512, 2047 - t0 - z0)
                p.dma("sync",
                      bass.AP(grd.tensor, grd.offset + z0, [[2176, 128], [1, zw]]),
                      zrs[:, 0:zw])
            p.dma("sync",
                  bass.AP(grd.tensor, grd.offset + 2047 - t0, [[2175, 128], [1, L]]),
                  attn[:, 0:L])
            for jt in range(qt + 1):
                sl = jt % 4
                p.op("tensor", lambda e, jt=jt, sl=sl: e.transpose(
                    psbf[0:128, sl * 128:sl * 128 + 128],
                    attn[:, jt * 128:jt * 128 + 128], idb[:, :]))
                p.op("vector", lambda e, sl=sl: e.tensor_copy(
                    attT[:, sl * 128:sl * 128 + 128],
                    psbf[0:128, sl * 128:sl * 128 + 128]))
                p.op("tensor", lambda e, jt=jt, sl=sl, h=h, hb=hb: e.matmul(
                    ps[hb:hb + 64, 2048:2176],
                    bass.AP(vtm, jt * 1024 + h * 64, [[16 * 1024, 128], [1, 64]]),
                    attT[:, sl * 128:sl * 128 + 128],
                    start=(jt == 0), stop=False))
            p.dma("sync", gsb[:, 0:2048],
                  bass.AP(grd.tensor, grd.offset, [[2176, 128], [1, 2048]]))
            for ci in range(15 - qt, 16):
                sl = ci % 4
                p.op("tensor", lambda e, ci=ci, sl=sl: e.transpose(
                    psbf[0:128, sl * 128:sl * 128 + 128],
                    gsb[:, ci * 128:ci * 128 + 128], idb[:, :]))
                p.op("vector", lambda e, sl=sl: e.tensor_copy(
                    attT[:, sl * 128:sl * 128 + 128],
                    psbf[0:128, sl * 128:sl * 128 + 128]))
                p.op("tensor", lambda e, ci=ci, sl=sl, hb=hb: e.matmul(
                    ps[hb:hb + 64, 2048:2176], evv[:, ci, 0:64],
                    attT[:, sl * 128:sl * 128 + 128],
                    start=False, stop=(ci == 15)))
            p.op("scalar", lambda e, hb=hb, hc=hc, t0=t0: e.copy(
                bass.AP(qb, hb * NCH * T + hc * T + t0, [[NCH * T, 64], [1, 128]]),
                ps[hb:hb + 64, 2048:2176]))
    fo = V3(kb, NCH, T)
    linear(p, "o", p.t["wproj"], D, D, qv, fo, T, wbuf, ps[:, 0:512])
    return fo


def run_l2_attn(x, inp):
    nc = _get_nc("l2_attn", lambda: build_half(body_l2_attn, decl_l2_attn, "l2a",
                                               alloc_f=False))
    wa = np.asarray(inp["L2_attn_w_attn"], np.float32)
    embk = np.asarray(inp["L2_attn_embk"], np.float32)
    embv = np.asarray(inp["L2_attn_embv"], np.float32)
    ek = np.concatenate([embk, np.tile(embk[-1:], (1024, 1))], axis=0)
    ev = np.concatenate([embv, np.tile(embv[-1:], (1024, 1))], axis=0)
    ekr_pad = np.zeros((2176, 64), np.float32)
    ekr_pad[0:2048] = ek[::-1]
    evr = np.ascontiguousarray(ev[::-1])
    com = {
        "wq2": np.ascontiguousarray(wa[:, 0:D]),
        "wk2": np.ascontiguousarray(wa[:, D:2 * D]),
        "wv2": np.ascontiguousarray(wa[:, 2 * D:3 * D]),
        "wproj": np.ascontiguousarray(inp["L2_attn_w_proj"], np.float32),
        "ekrT": np.ascontiguousarray(np.tile(ekr_pad.T, (2, 1))),
        "evr": evr,
        "trilm": np.tril(np.ones((128, 128), np.float32)),
        "ident": np.eye(128, dtype=np.float32),
    }
    in_maps = [dict(com, x_in=_fm(x[j])) for j in range(NCORES)]
    res = run_bass_kernel_spmd(nc, in_maps, core_ids=list(range(NCORES)))
    return np.stack([res.results[j]["x_out"].T for j in range(NCORES)])


def _fm(a):
    return np.ascontiguousarray(np.asarray(a, dtype=np.float32).T)


def _sel_l3():
    s = np.zeros((16, NCH * 128), np.float32)
    for c in range(NCH):
        for pp in range(128):
            g = 2 * c + pp // 64
            s[g, c * 128 + pp] = 1.0
    return s


_CACHE = {}


def _get_nc(key, builder):
    if key not in _CACHE:
        _CACHE[key] = builder()
    return _CACHE[key]


def run_l3_attn(x, inp):
    nc = _get_nc("l3_attn", lambda: build_half(body_l3_attn, decl_l3_attn, "l3a"))
    com = {
        "wq": np.ascontiguousarray(inp["L3_attn_Wq"], np.float32),
        "wk": np.ascontiguousarray(inp["L3_attn_Wk"], np.float32),
        "wv": np.ascontiguousarray(inp["L3_attn_Wv"], np.float32),
        "wo": np.ascontiguousarray(inp["L3_attn_Wo"], np.float32),
        "bq": np.ascontiguousarray(inp["L3_attn_bq"], np.float32),
        "bk": np.ascontiguousarray(inp["L3_attn_bk"], np.float32),
        "bv": np.ascontiguousarray(inp["L3_attn_bv"], np.float32),
        "bo": np.ascontiguousarray(inp["L3_attn_bo"], np.float32),
        "sel": _sel_l3(),
        "ident": np.eye(128, dtype=np.float32),
    }
    in_maps = [dict(com, x_in=_fm(x[j])) for j in range(NCORES)]
    res = run_bass_kernel_spmd(nc, in_maps, core_ids=list(range(NCORES)))
    return np.stack([res.results[j]["x_out"].T for j in range(NCORES)])


def _xpos_tables():
    n, dh, win = T, 64, 512
    half = np.arange(0, dh, 2, dtype=np.float64)
    inv_freq = 1.0 / 10000.0 ** (half / dh)
    t = np.arange(n, dtype=np.float64)
    fr = t[:, None] * inv_freq[None, :]
    fr = np.concatenate([fr, fr], axis=1)
    base = (half + 0.4 * dh) / (1.4 * dh)
    power = (t - n // 2) / float(win // 2)
    sc = base[None, :] ** power[:, None]
    sc = np.concatenate([sc, sc], axis=1)
    cos, sin = np.cos(fr), np.sin(fr)
    csq = (cos * sc).T; ssq = (sin * sc).T
    csk = (cos / sc).T; ssk = (sin / sc).T
    tile = lambda a: np.ascontiguousarray(np.tile(a, (2, 1)), dtype=np.float32)
    return tile(csq), tile(ssq), tile(csk), tile(ssk)


def _rotm():
    r = np.zeros((128, 128), np.float32)
    for blk in (0, 64):
        for dp in range(32):
            r[blk + dp + 32, blk + dp] = -1.0
        for dp in range(32, 64):
            r[blk + dp - 32, blk + dp] = 1.0
    return r


def _w1perm(w1, b1):
    a, g = w1[:, :HID], w1[:, HID:]
    ba, bg = b1[:HID], b1[HID:]
    blocks = []
    for i in range(NSLAB):
        aw = min(128, HID - i * 128)
        blocks.append(a[:, i * 128:i * 128 + aw])
        blocks.append(g[:, i * 128:i * 128 + aw])
    return (np.ascontiguousarray(np.concatenate(blocks, axis=1), np.float32),
            np.ascontiguousarray(ba, np.float32),
            np.ascontiguousarray(bg, np.float32))


def run_ffn_tpl(x, inp, pref):
    nc = _get_nc("ffn_tpl", lambda: build_half(body_ffn, decl_ffn, "ffn",
                                               alloc_f=False))
    csq, ssq, csk, ssk = _xpos_tables()
    bm = np.concatenate([np.triu(np.ones((128, 128), np.float32)),
                         np.tril(np.ones((128, 128), np.float32))], axis=1)
    bexph = np.zeros((2, 128), np.float32)
    bexph[0, 0:64] = 1.0
    bexph[1, 64:128] = 1.0
    com = {"rotm": _rotm(), "csq": csq, "ssq": ssq, "csk": csk, "ssk": ssk,
           "bandmask": bm, "ident": np.eye(128, dtype=np.float32),
           "bexph": bexph}
    for l in range(2):
        wqkv = np.asarray(inp[f"{pref}_wqkv"][l], np.float32)
        com[f"wq{l}"] = np.ascontiguousarray(wqkv[:, 0:512])
        com[f"wk{l}"] = np.ascontiguousarray(wqkv[:, 512:1024])
        com[f"wv{l}"] = np.ascontiguousarray(wqkv[:, 1024:1536])
        com[f"wo{l}"] = np.ascontiguousarray(inp[f"{pref}_wo"][l], np.float32)
        w1p, b1a, b1g = _w1perm(np.asarray(inp[f"{pref}_w1"][l], np.float32),
                                np.asarray(inp[f"{pref}_b1"][l], np.float32))
        com[f"w1p{l}"] = w1p
        com[f"b1a{l}"] = b1a
        com[f"b1g{l}"] = b1g
        com[f"w2{l}"] = np.ascontiguousarray(inp[f"{pref}_w2"][l], np.float32)
        com[f"b2{l}"] = np.ascontiguousarray(inp[f"{pref}_b2"][l], np.float32)
        com[f"qs{l}"] = np.ascontiguousarray(
            np.tile(np.asarray(inp[f"{pref}_q_scale"][l], np.float32), 2))
        com[f"ks{l}"] = np.ascontiguousarray(
            np.tile(np.asarray(inp[f"{pref}_k_scale"][l], np.float32), 2))
        com[f"lng{l}"] = np.ascontiguousarray(inp[f"{pref}_ln_g"][l], np.float32)
        com[f"lnb{l}"] = np.ascontiguousarray(inp[f"{pref}_ln_b"][l], np.float32)
    in_maps = [dict(com, x_in=_fm(x[j])) for j in range(NCORES)]
    res = run_bass_kernel_spmd(nc, in_maps, core_ids=list(range(NCORES)))
    return np.stack([res.results[j]["x_out"].T for j in range(NCORES)])


def run_l0_ffn(x, inp):
    return run_ffn_tpl(x, inp, "L0_ffn")


def run_l2_ffn(x, inp):
    return run_ffn_tpl(x, inp, "L2_ffn")


def run_l0_attn(x, inp):
    nc = _get_nc("l0_attn", lambda: build_half(body_l0_attn, decl_l0_attn, "l0a",
                                               alloc_f=False))
    m = np.zeros((64, 64), np.float32)
    for j in range(64):
        for i in range(64):
            if j // 16 == i // 16 and j <= i:
                m[j, i] = 1.0
    com = {
        "wkT": _fm(inp["L0_attn_Wk"]),
        "wvT": _fm(inp["L0_attn_Wv"]),
        "wrT": _fm(inp["L0_attn_Wr"]),
        "woT": _fm(inp["L0_attn_Wo"]),
        "tdec": np.ascontiguousarray(inp["L0_attn_time_decay"], np.float32),
        "tfir": np.ascontiguousarray(inp["L0_attn_time_first"], np.float32),
        "m64": np.ascontiguousarray(np.tile(m, (2, 1))),
        "ones64": np.ones((64, 64), np.float32),
        "ident": np.eye(128, dtype=np.float32),
    }
    in_maps = [dict(com, x_in=_fm(x[j])) for j in range(NCORES)]
    res = run_bass_kernel_spmd(nc, in_maps, core_ids=list(range(NCORES)))
    return np.stack([res.results[j]["x_out"].T for j in range(NCORES)])


def _int3(w):
    blocks = []
    for i in range(8):
        blocks.append(w[..., 0 * D + i * 128: 0 * D + i * 128 + 128])
        blocks.append(w[..., 1 * D + i * 128: 1 * D + i * 128 + 128])
        blocks.append(w[..., 2 * D + i * 128: 2 * D + i * 128 + 128])
    return np.ascontiguousarray(np.concatenate(blocks, axis=-1), np.float32)


def _sel8():
    s = np.zeros((8, 4 * 128), np.float32)
    for c in range(4):
        for pp in range(128):
            s[2 * c + pp // 64, c * 128 + pp] = 1.0
    return s


def run_l3_ffn(x, inp):
    nc = _get_nc("l3_ffn", lambda: build_half(body_l3_ffn, decl_l3_ffn, "l3f"))
    ew1 = np.asarray(inp["L3_ffn_expert_w1"], np.float32)
    w1all = np.ascontiguousarray(np.concatenate([ew1[e] for e in range(8)], axis=1))
    b1all = np.ascontiguousarray(np.asarray(inp["L3_ffn_expert_b1"], np.float32).reshape(-1))
    ew2 = np.asarray(inp["L3_ffn_expert_w2"], np.float32)
    w2stack = np.concatenate([ew2[e] for e in range(8)], axis=0)
    com = {
        "iw": np.ascontiguousarray(inp["L3_ffn_integration_weight"], np.float32),
        "mu": np.ascontiguousarray(inp["L3_ffn_mu"], np.float32),
        "rw1": np.ascontiguousarray(inp["L3_ffn_router_w1"], np.float32),
        "rb1": np.ascontiguousarray(inp["L3_ffn_router_b1"], np.float32),
        "rw2": np.ascontiguousarray(inp["L3_ffn_router_w2"], np.float32),
        "rb2": np.ascontiguousarray(inp["L3_ffn_router_b2"], np.float32).reshape(1, 8),
        "hw1": np.ascontiguousarray(inp["L3_ffn_halt_w1"], np.float32),
        "hb1": np.ascontiguousarray(inp["L3_ffn_halt_b1"], np.float32),
        "hw2": np.ascontiguousarray(inp["L3_ffn_halt_w2"], np.float32),
        "hb2": np.ascontiguousarray(inp["L3_ffn_halt_b2"], np.float32),
        "w1all": w1all, "b1all": b1all,
        "w2int": _int3(w2stack),
        "b2eint": _int3(np.asarray(inp["L3_ffn_expert_b2"], np.float32)),
        "shw1": np.ascontiguousarray(inp["L3_ffn_shared_w1"], np.float32),
        "shb1": np.ascontiguousarray(inp["L3_ffn_shared_b1"], np.float32),
        "shw2": _int3(np.ascontiguousarray(inp["L3_ffn_shared_w2"], np.float32)),
        "shb2": _int3(np.ascontiguousarray(inp["L3_ffn_shared_b2"], np.float32)),
        "swv": np.ascontiguousarray(inp["L3_ffn_shared_weight"], np.float32).reshape(1),
        "rfw1": np.ascontiguousarray(inp["L3_ffn_refine_w1"], np.float32),
        "rfb1": np.ascontiguousarray(inp["L3_ffn_refine_b1"], np.float32),
        "rfw2": np.ascontiguousarray(inp["L3_ffn_refine_w2"], np.float32),
        "rfb2": np.ascontiguousarray(inp["L3_ffn_refine_b2"], np.float32),
        "sel8": _sel8(),
        "ident": np.eye(128, dtype=np.float32),
    }
    in_maps = [dict(com, x_in=_fm(x[j])) for j in range(NCORES)]
    res = run_bass_kernel_spmd(nc, in_maps, core_ids=list(range(NCORES)))
    return np.stack([res.results[j]["x_out"].T for j in range(NCORES)])


def run_l1_attn(x, inp):
    nc = _get_nc("l1_attn", lambda: build_half(body_l1_attn, decl_l1_attn, "l1a"))
    bm = np.concatenate([np.triu(np.ones((128, 128), np.float32)),
                         np.tril(np.ones((128, 128), np.float32))], axis=1)
    com = {
        "lng": np.ascontiguousarray(inp["L1_attn_ln_gamma"], np.float32),
        "lnb": np.ascontiguousarray(inp["L1_attn_ln_beta"], np.float32),
        "wq1": np.ascontiguousarray(inp["L1_attn_w_qkv"][:, 0:D], np.float32),
        "wk1": np.ascontiguousarray(inp["L1_attn_w_qkv"][:, D:2 * D], np.float32),
        "wv1": np.ascontiguousarray(inp["L1_attn_w_qkv"][:, 2 * D:3 * D], np.float32),
        "wout": np.ascontiguousarray(inp["L1_attn_w_out"], np.float32),
        "bandmask": bm,
        "ident": np.eye(128, dtype=np.float32),
    }
    in_maps = [dict(com, x_in=_fm(x[j])) for j in range(NCORES)]
    res = run_bass_kernel_spmd(nc, in_maps, core_ids=list(range(NCORES)))
    return np.stack([res.results[j]["x_out"].T for j in range(NCORES)])


def run_l1_ffn(x, inp):
    nc = _get_nc("l1_ffn", lambda: build_half(body_l1_ffn, decl_l1_ffn, "l1f"))
    com = {
        "wf": _fm(inp["L1_ffn_wf"]), "uf": _fm(inp["L1_ffn_uf"]),
        "wi": _fm(inp["L1_ffn_wi"]), "ui": _fm(inp["L1_ffn_ui"]),
        "wc": _fm(inp["L1_ffn_wc"]), "uc": _fm(inp["L1_ffn_uc"]),
        "wo": _fm(inp["L1_ffn_wo"]), "uo": _fm(inp["L1_ffn_uo"]),
        "bf": np.ascontiguousarray(inp["L1_ffn_bf"], np.float32),
        "bi": np.ascontiguousarray(inp["L1_ffn_bi"], np.float32),
        "bc": np.ascontiguousarray(inp["L1_ffn_bc"], np.float32),
        "bo": np.ascontiguousarray(inp["L1_ffn_bo"], np.float32),
        "in_c": np.ascontiguousarray(inp["L1_ffn_in_c"], np.float32),
    }
    in_maps = []
    for j in range(NCORES):
        sl = np.asarray(x[:, j * 256:(j + 1) * 256, :], np.float32)
        xc = sl.reshape(8 * 256, D)
        in_maps.append(dict(com, x_in=_fm(xc)))
    res = run_bass_kernel_spmd(nc, in_maps, core_ids=list(range(NCORES)))
    out = np.empty_like(np.asarray(x, np.float32))
    for j in range(NCORES):
        o = res.results[j]["x_out"].T.reshape(8, 256, D)
        out[:, j * 256:(j + 1) * 256, :] = o
    return out


def kernel(**inputs):
    x = np.asarray(inputs["L0_attn_x"], dtype=np.float32)
    x = run_l0_attn(x, inputs)
    x = run_l0_ffn(x, inputs)
    x = run_l1_attn(x, inputs)
    x = run_l1_ffn(x, inputs)
    x = run_l2_attn(x, inputs)
    x = run_l2_ffn(x, inputs)
    x = run_l3_attn(x, inputs)
    x = run_l3_ffn(x, inputs)
    return x
